# version 18
# baseline (speedup 1.0000x reference)
import contextlib
import numpy as np

import concourse.bass as bass
import concourse.mybir as mybir
from concourse.bass_utils import run_bass_kernel_spmd

F32 = mybir.dt.float32
F16 = mybir.dt.float16
AF = mybir.ActivationFunctionType
ALU = mybir.AluOpType

D = 2048
NCH = 16
NH = 16
HD = 128
BLK = 256
HF = 8192
HCH = 64
WS = 64.0
IWS = 1.0 / WS

ENGS = ("t", "v", "s", "q", "g")
SEMS = ("t", "v", "s", "qc", "qx", "qx1", "qw0", "qw1", "qw2", "qw3", "qu0", "qu1", "g", "gs0", "gs1", "gs2", "gs3", "gq0", "gq1", "gq2", "gq3", "gq4", "gq5", "gq6", "gq7", "gw0", "gw1", "gw2", "gw3", "qm0", "qm1")
DMA_SEMS = ("qc", "qx", "qx1", "qw0", "qw1", "qw2", "qw3", "qu0", "qu1", "g", "gs0", "gs1", "gs2", "gs3", "gq0", "gq1", "gq2", "gq3", "gq4", "gq5", "gq6", "gq7", "gw0", "gw1", "gw2", "gw3", "qm0", "qm1")


class Buf:
    __slots__ = ("ready", "users")

    def __init__(self):
        self.ready = []
        self.users = []

    def r(self):
        return list(self.ready)

    def w(self):
        return list(self.users) + list(self.ready)

    def mark_user(self, mark):
        self.users.append(mark)

    def mark_ready(self, mark):
        self.ready = [mark]
        self.users = []


class P:
    def __init__(self):
        self.ops = {e: [] for e in ENGS}
        self.cnt = {sm: 0 for sm in SEMS}
        self.hw = {e: {sm: 0 for sm in SEMS} for e in ENGS}

    def _wait(self, eng, dep):
        f, val = dep
        if val <= 0:
            return
        if eng == "t" and f == "t":
            return
        if self.hw[eng][f] >= val:
            return
        self.hw[eng][f] = val
        self.ops[eng].append(("w", f, val))

    def deps(self, eng, reads=(), writes=(), extra=()):
        acc = {}
        for b in reads:
            for f, v in b.r():
                acc[f] = max(acc.get(f, 0), v)
        for b in writes:
            for f, v in b.w():
                acc[f] = max(acc.get(f, 0), v)
        for f, v in extra:
            acc[f] = max(acc.get(f, 0), v)
        for f, v in acc.items():
            self._wait(eng, (f, v))

    def raw(self, eng, fn):
        self.ops[eng].append(("i", fn, None))

    def inc(self, eng, fn, sem=None):
        sm = sem if sem is not None else ("g" if eng == "g" else eng)
        assert not (eng == "q" and sem is None), "sync-queue ops must name a DMA sem"
        step = 16 if sm in DMA_SEMS else 1
        self.cnt[sm] += step
        self.ops[eng].append(("i", fn, sm))
        return (sm, self.cnt[sm])

    def op(self, eng, fn, reads=(), writes=(), extra=(), sem=None):
        self.deps(eng, reads, writes, extra)
        mark = self.inc(eng, fn, sem)
        for b in reads:
            b.mark_user(mark)
        for b in writes:
            b.mark_ready(mark)
        return mark

    def mm_group(self, fns, reads=(), writes=(), extra=()):
        self.deps("t", reads, writes, extra)
        for f in fns[:-1]:
            self.raw("t", f)
        mark = self.inc("t", fns[-1])
        for b in reads:
            b.mark_user(mark)
        for b in writes:
            b.mark_ready(mark)
        return mark

    def barrier(self):
        snap = dict(self.cnt)
        for e in ENGS:
            for sm in SEMS:
                self._wait(e, (sm, snap[sm]))


def build(T, stages="all"):
    NBLK = T // BLK
    nc = bass.Bass("TRN2", target_bir_lowering=False)

    xT = nc.dram_tensor("xT", [D, T], F32, kind="ExternalInput").ap()
    wq = nc.dram_tensor("wq", [D, 3 * D], F32, kind="ExternalInput").ap()
    wg = nc.dram_tensor("wg", [D, D], F32, kind="ExternalInput").ap()
    wo = nc.dram_tensor("wo", [D, D], F32, kind="ExternalInput").ap()
    w1 = nc.dram_tensor("w1", [D, HF], F32, kind="ExternalInput").ap()
    w3 = nc.dram_tensor("w3", [D, HF], F32, kind="ExternalInput").ap()
    w2t = nc.dram_tensor("w2t", [HF, D], F32, kind="ExternalInput").ap()
    wmod = nc.dram_tensor("wmod", [D, 6 * D], F32, kind="ExternalInput").ap()
    cT = nc.dram_tensor("cT", [128, NCH], F32, kind="ExternalInput").ap()
    scols = nc.dram_tensor("scols", [128, NH], F32, kind="ExternalInput").ap()
    normw = nc.dram_tensor("normw", [128, NCH], F32, kind="ExternalInput").ap()
    identc = nc.dram_tensor("identc", [128, 128], F32, kind="ExternalInput").ap()
    rampc = nc.dram_tensor("rampc", [128, BLK], F32, kind="ExternalInput").ap()
    idx0c = nc.dram_tensor("idx0c", [128, BLK], F32, kind="ExternalInput").ap()
    idx1c = nc.dram_tensor("idx1c", [128, BLK], F32, kind="ExternalInput").ap()
    kcolc = nc.dram_tensor("kcolc", [128, 2], F32, kind="ExternalInput").ap()
    h128c = nc.dram_tensor("h128c", [128, 128], F32, kind="ExternalInput").ap()
    h64dc = nc.dram_tensor("h64dc", [128, 128], F32, kind="ExternalInput").ap()
    yT = nc.dram_tensor("yT", [D, T], F32, kind="ExternalOutput").ap()
    x2d = nc.dram_tensor("x2d", [D, T], F32, kind="Internal").ap()
    tAd = nc.dram_tensor("tAd", [HF, D], F32, kind="Internal").ap()
    w2fd = nc.dram_tensor("w2fd", [HF, D], F32, kind="Internal").ap()
    wq_s = nc.dram_tensor("wq_s", [48, 128, 4096], F16, kind="Internal").ap()
    wg_s = nc.dram_tensor("wg_s", [16, 128, 4096], F16, kind="Internal").ap()
    wo_s = nc.dram_tensor("wo_s", [16, 128, 4096], F16, kind="Internal").ap()
    w1_s = nc.dram_tensor("w1_s", [64, 128, 4096], F16, kind="Internal").ap()
    w3_s = nc.dram_tensor("w3_s", [64, 128, 4096], F16, kind="Internal").ap()
    w2f_s = nc.dram_tensor("w2f_s", [HF, 2 * D], F16, kind="Internal").ap()

    p = P()

    with contextlib.ExitStack() as es:
        E = es.enter_context
        identt = E(nc.sbuf_tensor("identt", [128, 128], F32))
        onest = E(nc.sbuf_tensor("onest", [128, 128], F32))
        rampt = E(nc.sbuf_tensor("rampt", [128, BLK], F32))
        idx0t = E(nc.sbuf_tensor("idx0t", [128, BLK], F32))
        idx1t = E(nc.sbuf_tensor("idx1t", [128, BLK], F32))
        kcolt = E(nc.sbuf_tensor("kcolt", [128, 2], F32))
        h128t = E(nc.sbuf_tensor("h128t", [128, 128], F32))
        h64dt = E(nc.sbuf_tensor("h64dt", [128, 128], F32))
        scolt = E(nc.sbuf_tensor("scolt", [128, NH], F32))
        negst = E(nc.sbuf_tensor("negst", [128, NH], F32))
        kdect = E(nc.sbuf_tensor("kdect", [128, 2 * NH], F32))
        bdct = E(nc.sbuf_tensor("bdct", [128, NH], F32))
        cst = E(nc.sbuf_tensor("cst", [128, NCH], F32))
        modct = E(nc.sbuf_tensor("modct", [128, 96], F32))
        capc = E(nc.sbuf_tensor("capc", [128, NCH], F32))
        cbpc = E(nc.sbuf_tensor("cbpc", [128, NCH], F32))
        gac = E(nc.sbuf_tensor("gac", [128, NCH], F32))
        gbc = E(nc.sbuf_tensor("gbc", [128, NCH], F32))
        tmga = E(nc.sbuf_tensor("tmga", [128, NCH], F32))
        tmgb = E(nc.sbuf_tensor("tmgb", [128, NCH], F32))
        normwt = E(nc.sbuf_tensor("normwt", [128, NCH], F32))
        eps5c = E(nc.sbuf_tensor("eps5c", [128, 1], F32))
        eps6x = E(nc.sbuf_tensor("eps6x", [128, 1], F32))
        xb = E(nc.sbuf_tensor("xb", [128, NCH * BLK], F32))
        xbB = E(nc.sbuf_tensor("xbB", [128, NCH * BLK], F32))
        scrA = E(nc.sbuf_tensor("scrA", [128, 8 * BLK], F32))
        hz = E(nc.sbuf_tensor("hz", [128, NCH * BLK], F32))
        hsp = E(nc.sbuf_tensor("hsp", [128, 2 * NCH * BLK], F16))
        arQ = E(nc.sbuf_tensor("arQ", [128, 2 * NCH * BLK], F32))
        arK = E(nc.sbuf_tensor("arK", [128, 2 * NCH * BLK], F32))
        ar3 = E(nc.sbuf_tensor("ar3", [128, 4608], F32))
        rowt = E(nc.sbuf_tensor("rowt", [1, 1536], F32))
        rstdR = E(nc.sbuf_tensor("rstdR", [128, BLK], F32))
        mrR = E(nc.sbuf_tensor("mrR", [128, BLK], F32))
        rmsR = E(nc.sbuf_tensor("rmsR", [128, BLK], F32))
        wsl16 = [E(nc.sbuf_tensor(f"wsl16_{i}", [128, 2 * NCH * BLK], F16)) for i in range(2)]
        pb = [E(nc.psum_tensor(f"pb{i}", [128, 512], F32)) for i in range(8)]

        sems = {sm: E(nc.semaphore(f"sem_{sm}")) for sm in SEMS}
        block = E(nc.Block())

        q_t = arQ.ap()[:, 0 : NCH * BLK]
        gate_t = arQ.ap()[:, NCH * BLK : 2 * NCH * BLK]
        A_t = arQ.ap()
        pA = arQ.ap()[:, 0 : NCH * BLK]
        pB = arQ.ap()[:, NCH * BLK : 2 * NCH * BLK]
        pC = arK.ap()[:, 0 : NCH * BLK]
        pD = arK.ap()[:, NCH * BLK : 2 * NCH * BLK]
        k_t = arK.ap()[:, 0 : NCH * BLK]
        vt0 = arK.ap()[:, NCH * BLK : NCH * BLK + NH * HD]
        vt1 = arK.ap()[:, NCH * BLK + NH * HD : NCH * BLK + 2 * NH * HD]
        B_t = arK.ap()
        Ah_t = arK.ap()[:, 0 : NCH * BLK].bitcast(F16)
        Al_t = arK.ap()[:, NCH * BLK : 2 * NCH * BLK].bitcast(F16)
        hh_t = hsp.ap()[:, 0 : NCH * BLK]
        hl_t = hsp.ap()[:, NCH * BLK : 2 * NCH * BLK]
        kv_t = ar3.ap()[:, 0:2048]
        slot_off = (2048, 3328)
        ya_t = ar3.ap()[:, 0:4096]

        bufs = {}

        def mkbuf(*names):
            for n in names:
                bufs[n] = Buf()

        mkbuf("consts", "cst", "modct", "modpost", "modctB", "modpostB", "xb0", "xb1", "hz", "hs", "gate",
              "q", "k", "vt0", "vt1", "rstdR", "mrR", "rmsR",
              "r_sum", "r_sq", "r_a", "r_b", "r_rstd", "r_mr",
              "pA", "pB", "pC", "pD", "wslh0", "wslh1", "wsll0", "wsll1",
              "w2s0", "w2s1", "decays", "ya", "A", "Bf", "Asp")
        for i in range(8):
            mkbuf(f"pb{i}", f"scr{i}")
        for h in range(NH):
            mkbuf(f"kv{h}")
        for i in range(2):
            mkbuf(f"slot{i}")
        bufs["r_rsq"] = bufs["r_sq"]
        bufs["r_rtmp"] = bufs["r_b"]
        bufs["r_rrstd"] = bufs["r_rstd"]
        for b_ in range(NBLK):
            mkbuf(f"x2_{b_}")
        mkbuf("tAd", "w2fd")
        for w in ("wq", "wg", "wo", "w1", "w3", "w2f"):
            mkbuf(w + "S")

        ROW = {"r_sum": 0, "r_sq": 256, "r_a": 512, "r_b": 768, "r_rstd": 1024,
               "r_mr": 1280, "r_rsq": 256, "r_rtmp": 768, "r_rrstd": 1024}

        def row(n):
            return rowt.ap()[:, ROW[n] : ROW[n] + 256]

        def scr(i):
            return scrA.ap()[:, (i % 8) * BLK : (i % 8 + 1) * BLK]

        def scrb(i):
            return bufs[f"scr{i % 8}"]

        const_pairs = [(identt, identc), (rampt, rampc), (idx0t, idx0c), (idx1t, idx1c),
                       (kcolt, kcolc), (h128t, h128c),
                       (h64dt, h64dc), (scolt, scols), (normwt, normw)]
        for dst, srcc in const_pairs:
            p.op("q", lambda h, d=dst, s=srcc: h.dma_start(out=d.ap()[:], in_=s[:]),
                 writes=[bufs["consts"]], sem="qc")
        p.op("q", lambda h: h.dma_start(out=scrA.ap()[:, 0:NCH], in_=cT[:]),
             writes=[bufs["scr0"]], sem="qc")
        qcfin = ("qc", p.cnt["qc"])
        bufs["consts"].mark_ready(qcfin)
        bufs["scr0"].mark_ready(qcfin)
        p.op("v", lambda h: h.memset(onest.ap()[:], 1.0), writes=[bufs["consts"]])
        p.op("v", lambda h: h.memset(eps5c.ap()[:], 1e-5), writes=[bufs["consts"]])
        p.op("v", lambda h: h.memset(eps6x.ap()[:], 1e-6 * WS * WS), writes=[bufs["consts"]])
        p.op("v", lambda h: h.memset(kv_t[:, :], 0.0), writes=[bufs[f"kv{hh}"] for hh in range(NH)])
        p.op("v", lambda h: h.memset(rowt.ap()[:, 0:96], 0.0), writes=[bufs["r_sum"]])
        p.op("s", lambda h: h.mul(out=negst.ap()[:], in_=scolt.ap()[:], mul=-1.0),
             reads=[bufs["consts"]], writes=[bufs["decays"]])
        p.op("s", lambda h: h.activation(out=cst.ap()[:], in_=scrA.ap()[:, 0:NCH], func=AF.Silu),
             reads=[bufs["scr0"]], writes=[bufs["cst"]])
        for h_ in range(NH):
            p.op("s", lambda h, i=h_: h.activation(out=kdect.ap()[:, 2 * i : 2 * i + 2],
                                                   in_=kcolt.ap()[:], func=AF.Exp,
                                                   scale=negst.ap()[:, i : i + 1]),
                 reads=[bufs["consts"], bufs["decays"]], writes=[bufs["decays"]])
            p.op("s", lambda h, i=h_: h.activation(out=bdct.ap()[:, i : i + 1],
                                                   in_=scolt.ap()[:, i : i + 1], func=AF.Exp,
                                                   scale=-float(BLK)),
                 reads=[bufs["consts"]], writes=[bufs["consts" if False else "decays"]])

        def emit_split_slab(src, j, dst, stsem=None):
                si = j % 2
                if stsem is None:
                    stsem = f"gs{si}"
                sl = pC if si == 0 else pD
                slb = bufs["pC" if si == 0 else "pD"]
                ob = wsl16[si]
                obb = bufs[f"wslh{si}"]
                obb2 = bufs[f"wsll{si}"]
                m1 = p.op("q", lambda h, d=sl, s=src, c=j * 128: h.dma_start(
                    out=d[:, 0:2048].rearrange("p (c w) -> p c w", c=NCH),
                    in_=s.rearrange("(c p) x -> p c x", p=128)[:, :, c : c + 128]),
                    writes=[slb], sem=f"qu{si}")
                p.op("s", lambda h, d=sl, o=ob: h.activation(
                    out=o.ap()[:, 0:2048], in_=d[:, 0:2048], func=AF.Identity, scale=WS),
                    reads=[slb], writes=[obb, obb2], extra=[m1])
                mstt = p.op("v", lambda h, d=sl, o=ob: h.scalar_tensor_tensor(
                    out=o.ap()[:, 2048:4096], in0=d[:, 0:2048], scalar=WS, in1=o.ap()[:, 0:2048],
                    op0=ALU.mult, op1=ALU.subtract),
                    reads=[slb, obb], writes=[obb, obb2])
                p.op("g", lambda h, o=ob, dd=dst, j_=j: h.dma_start(out=dd[j_, :, :], in_=o.ap()[:, 0:4096]),
                     reads=[obb, obb2], writes=[], sem=stsem)
                return si, mstt

        def emit_split(src, nslab, dst, wsbuf, semfn=None, qbufs=None, qsize=0):
            qsems = set()
            for j in range(nslab):
                sm = semfn(j) if semfn else None
                emit_split_slab(src, j, dst, stsem=sm)
                if sm is not None:
                    qsems.add(sm)
                if qbufs is not None and (j + 1) % qsize == 0:
                    qb = Buf()
                    qb.ready = [(s_, p.cnt[s_]) for s_ in sorted(qsems)]
                    qbufs.append(qb)
            if semfn is None:
                wsbuf.ready = [("gs0", p.cnt["gs0"]), ("gs1", p.cnt["gs1"])]
            else:
                sms = sorted({semfn(j) for j in range(nslab)})
                wsbuf.ready = [(sm, p.cnt[sm]) for sm in sms]

        def foldA_iter(uc):
            si = uc % 2
            sl = pA if si == 0 else pB
            slb = bufs["pA" if si == 0 else "pB"]
            ld = p.op("q", lambda h, d=sl, r0=uc * 128:
                      h.dma_start(out=d[:, 0:2048], in_=w2t[r0 : r0 + 128, :]),
                      writes=[slb], sem=f"qw{si}")
            ebuf = scrA.ap()[:, 0:2048] if uc % 2 == 0 else hz.ap()[:, 0:2048]
            ebufb = [bufs[f"scr{i}"] for i in range(8)] if uc % 2 == 0 else [bufs["hz"]]
            for dt_ in range(4):
                pbk = pb[dt_ % 2]
                pbkb = bufs[f"pb{dt_ % 2}"]
                p.mm_group([lambda h, d=sl, t0=dt_ * 512, pk=pbk:
                            h.matmul(pk.ap()[:, 0:512], h128t.ap()[:], d[:, t0 : t0 + 512],
                                     start=True, stop=True)],
                           reads=[slb, bufs["consts"]], writes=[pbkb], extra=[ld])
                if dt_ % 2 == 0:
                    p.op("v", lambda h, e=ebuf, t0=dt_ * 512, pk=pbk:
                         h.tensor_copy(out=e[:, t0 : t0 + 512], in_=pk.ap()[:, 0:512]),
                         reads=[pbkb], writes=ebufb)
                else:
                    p.op("s", lambda h, e=ebuf, t0=dt_ * 512, pk=pbk:
                         h.copy(out=e[:, t0 : t0 + 512], in_=pk.ap()[:, 0:512]),
                         reads=[pbkb], writes=ebufb)
            p.op("g", lambda h, e=ebuf, r0=uc * 128:
                 h.dma_start(out=tAd[r0 : r0 + 128, :], in_=e[:, 0:2048]),
                 reads=ebufb, writes=[bufs["tAd"]])

        def foldB_iter(vg):
            si = vg % 2
            sl = pA if si == 0 else pB
            slb = bufs["pA" if si == 0 else "pB"]
            l1 = p.op("q", lambda h, d=sl, v0=2 * vg:
                      h.dma_start(out=d[0:64, 0:2048],
                                  in_=tAd.rearrange("(u v) d -> v u d", v=128)[v0, :, :]),
                      reads=[bufs["tAd"]], writes=[slb], sem=f"qw{si}")
            l2 = p.op("q", lambda h, d=sl, v0=2 * vg + 1:
                      h.dma_start(out=d[64:128, 0:2048],
                                  in_=tAd.rearrange("(u v) d -> v u d", v=128)[v0, :, :]),
                      reads=[bufs["tAd"]], writes=[slb], sem=f"qw{si}")
            slb.ready = [l2]
            ebuf = scrA.ap()[:, 0:2048] if vg % 2 == 0 else hz.ap()[:, 0:2048]
            ebufb = [bufs[f"scr{i}"] for i in range(8)] if vg % 2 == 0 else [bufs["hz"]]
            fsp = hsp.ap()[:, 0:4096] if vg % 2 == 0 else hsp.ap()[:, 4096:8192]
            fspb = bufs["hs"]
            for dt_ in range(4):
                pbk = pb[2 + dt_ % 2]
                pbkb = bufs[f"pb{2 + dt_ % 2}"]
                p.mm_group([lambda h, d=sl, t0=dt_ * 512, pk=pbk:
                            h.matmul(pk.ap()[:, 0:512], h64dt.ap()[:], d[:, t0 : t0 + 512],
                                     start=True, stop=True)],
                           reads=[slb, bufs["consts"]], writes=[pbkb], extra=[l2])
                if dt_ % 2 == 0:
                    p.op("v", lambda h, e=ebuf, t0=dt_ * 512, pk=pbk:
                         h.tensor_copy(out=e[:, t0 : t0 + 512], in_=pk.ap()[:, 0:512]),
                         reads=[pbkb], writes=ebufb)
                else:
                    p.op("s", lambda h, e=ebuf, t0=dt_ * 512, pk=pbk:
                         h.copy(out=e[:, t0 : t0 + 512], in_=pk.ap()[:, 0:512]),
                         reads=[pbkb], writes=ebufb)
                p.op("s", lambda h, e=ebuf, f=fsp, t0=dt_ * 512: h.activation(
                    out=f[:, t0 : t0 + 512], in_=e[:, t0 : t0 + 512], func=AF.Identity, scale=WS),
                    reads=ebufb, writes=[fspb])
                p.op("v", lambda h, e=ebuf, f=fsp, t0=dt_ * 512: h.scalar_tensor_tensor(
                    out=f[:, 2048 + t0 : 2048 + t0 + 512], in0=e[:, t0 : t0 + 512], scalar=WS,
                    in1=f[:, t0 : t0 + 512], op0=ALU.mult, op1=ALU.subtract),
                    reads=ebufb + [fspb], writes=[fspb])
            p.op("g", lambda h, f=fsp, v0=2 * vg:
                 h.dma_start(out=w2f_s.rearrange("(u v) d -> v u d", v=128)[v0, :, :],
                             in_=f[0:64, :]),
                 reads=[fspb], writes=[], sem="gs2")
            p.op("g", lambda h, f=fsp, v0=2 * vg + 1:
                 h.dma_start(out=w2f_s.rearrange("(u v) d -> v u d", v=128)[v0, :, :],
                             in_=f[64:128, :]),
                 reads=[fspb], writes=[], sem="gs3")

        def emit_ln_stats(x_ap, xbuf, par7):
            inv_d = 1.0 / float(D)
            fns = [lambda h, d=dc_: h.matmul(pb[6].ap()[0:1, 0:256], onest.ap()[:, 0:1],
                                             x_ap[:, d * BLK : (d + 1) * BLK],
                                             start=(d == 0), stop=(d == NCH - 1))
                   for dc_ in range(NCH)]
            p.mm_group(fns, reads=[xbuf, bufs["consts"]], writes=[bufs["pb6"]])
            p.op("v", lambda h: h.tensor_copy(out=row("r_sum"), in_=pb[6].ap()[0:1, 0:256]),
                 reads=[bufs["pb6"]], writes=[bufs["r_sum"]])
            sqb = 7 if par7 else 6
            sqbb = bufs[f"pb{sqb}"]
            for half in range(2):
                for dcl in range(8):
                    dc = half * 8 + dcl
                    p.op("s", lambda h, d=dc: h.activation(out=scr(d), in_=x_ap[:, d * BLK : (d + 1) * BLK],
                                                           func=AF.Square),
                         reads=[xbuf], writes=[scrb(dc)])
                fns = [lambda h, d=half * 8 + dcl_: h.matmul(pb[sqb].ap()[0:1, 0:256], onest.ap()[:, 0:1],
                                                             scrA.ap()[:, (d % 8) * BLK : (d % 8 + 1) * BLK],
                                                             start=(d == 0), stop=(d == NCH - 1))
                       for dcl_ in range(8)]
                p.mm_group(fns, reads=[scrb(i) for i in range(8)] + [bufs["consts"]],
                           writes=([sqbb] if half == 0 else []))
            sqbb.mark_ready(("t", p.cnt["t"]))
            p.op("v", lambda h, b_sq=sqb: h.tensor_copy(out=row("r_sq"), in_=pb[b_sq].ap()[0:1, 0:256]),
                 reads=[sqbb], writes=[bufs["r_sq"]])
            p.op("s", lambda h: h.mul(out=row("r_a"), in_=row("r_sum"), mul=inv_d),
                 reads=[bufs["r_sum"]], writes=[bufs["r_a"]])
            p.op("v", lambda h: h.tensor_mul(out=row("r_b"), in0=row("r_a"), in1=row("r_a")),
                 reads=[bufs["r_a"]], writes=[bufs["r_b"]])
            p.op("v", lambda h: h.scalar_tensor_tensor(out=row("r_sum"), in0=row("r_sq"),
                                                       scalar=inv_d, in1=row("r_b"),
                                                       op0=ALU.mult, op1=ALU.subtract),
                 reads=[bufs["r_sq"], bufs["r_b"]], writes=[bufs["r_sum"]])
            p.op("s", lambda h: h.activation(out=row("r_b"), in_=row("r_sum"), func=AF.Sqrt,
                                             bias=eps5c.ap()[0:1, :]),
                 reads=[bufs["r_sum"], bufs["consts"]], writes=[bufs["r_b"]])
            p.op("v", lambda h: h.reciprocal(out=row("r_rstd"), in_=row("r_b")),
                 reads=[bufs["r_b"]], writes=[bufs["r_rstd"]])
            p.op("v", lambda h: h.tensor_mul(out=row("r_mr"), in0=row("r_a"), in1=row("r_rstd")),
                 reads=[bufs["r_a"], bufs["r_rstd"]], writes=[bufs["r_mr"]])
            p.mm_group([lambda h: h.matmul(pb[6].ap()[:, 0:256], onest.ap()[0:1, 0:128],
                                           row("r_rstd"), start=True, stop=True)],
                       reads=[bufs["r_rstd"], bufs["consts"]], writes=[bufs["pb6"]])
            p.op("v", lambda h: h.tensor_copy(out=rstdR.ap()[:], in_=pb[6].ap()[:, 0:256]),
                 reads=[bufs["pb6"]], writes=[bufs["rstdR"]])
            p.mm_group([lambda h, b_sq=sqb: h.matmul(pb[b_sq].ap()[:, 0:256], onest.ap()[0:1, 0:128],
                                                     row("r_mr"), start=True, stop=True)],
                       reads=[bufs["r_mr"], bufs["consts"]], writes=[sqbb])
            p.op("v", lambda h, b_sq=sqb: h.tensor_copy(out=mrR.ap()[:], in_=pb[b_sq].ap()[:, 0:256]),
                 reads=[sqbb], writes=[bufs["mrR"]])

        def emit_ln_apply(x_ap, xbuf, capcol, shift_off, mbufs=None):
            if mbufs is None:
                mbufs = [bufs["modpost"], bufs["modct"]]
            marks = []
            for dc in range(NCH):
                sl_ = slice(dc * BLK, (dc + 1) * BLK)
                p.op("v", lambda h, s=sl_, d=dc: h.tensor_mul(out=scr(d), in0=x_ap[:, s],
                                                              in1=rstdR.ap()[:]),
                     reads=[xbuf, bufs["rstdR"]], writes=[scrb(dc)])
                p.op("v", lambda h, d=dc: h.tensor_sub(out=scr(d), in0=scr(d), in1=mrR.ap()[:]),
                     reads=[scrb(dc), bufs["mrR"]], writes=[scrb(dc)])
                p.op("s", lambda h, s=sl_, d=dc, so=shift_off: h.activation(
                    out=hz.ap()[:, s], in_=scr(d), func=AF.Identity,
                    scale=capcol.ap()[:, d : d + 1], bias=modct.ap()[:, so + d : so + d + 1]),
                    reads=[scrb(dc)] + mbufs, writes=[bufs["hz"]])
                p.op("s", lambda h, s=sl_: h.copy(out=hh_t[:, s], in_=hz.ap()[:, s]),
                     reads=[bufs["hz"]], writes=[bufs["hs"]])
                m_ = p.op("v", lambda h, s=sl_: h.tensor_sub(out=hl_t[:, s], in0=hz.ap()[:, s],
                                                             in1=hh_t[:, s]),
                          reads=[bufs["hz"], bufs["hs"]], writes=[bufs["hs"]])
                marks.append(m_)
            return marks

        def load_x(par, b_idx, srcd, x2buf=None):
            dst = xb if par == 0 else xbB
            xbuf = bufs[f"xb{par}"]
            t = b_idx * BLK
            reads = [x2buf] if x2buf is not None else []
            sm = "qx" if par == 0 else "qx1"
            p.op("s", lambda h, t_=t, d=dst: h.dma_start(
                out=d.ap().rearrange("p (c w) -> p c w", c=NCH)[:, 0:8, :],
                in_=srcd.rearrange("(c p) x -> p c x", p=128)[:, 0:8, t_ : t_ + BLK]),
                reads=reads, writes=[xbuf], sem=sm)
            p.op("s", lambda h, t_=t, d=dst: h.dma_start(
                out=d.ap().rearrange("p (c w) -> p c w", c=NCH)[:, 8:16, :],
                in_=srcd.rearrange("(c p) x -> p c x", p=128)[:, 8:16, t_ : t_ + BLK]),
                reads=reads, writes=[xbuf], sem=sm)

        SLOTB = ("wslh0", "wslh1", "wsll0", "wsll1")

        def load_oc(slot, srcs, j, wsbuf):
            off = (slot % 2) * 4096
            ob = wsl16[slot // 2].ap()[:, off : off + 4096]
            obb = bufs[SLOTB[slot]]
            m1 = p.op("q", lambda h, d=ob, s=srcs, j_=j: h.dma_start(out=d, in_=s[j_, :, :]),
                      reads=[wsbuf], writes=[obb], sem=f"qw{slot}")
            return ob, obb, [m1]

        def mm_group_tracked(pk, pkb, wv, wb, lds, hmarks):
            p.deps("t", reads=[wb], writes=[pkb], extra=lds)
            for dc in range(NCH):
                p._wait("t", hmarks[dc])
                w0 = dc * 128
                hs_ = slice(dc * 256, dc * 256 + 256)
                first = dc == 0
                last = dc == NCH - 1
                p.raw("t", lambda h, d=wv, w=w0, s=hs_, f=first: h.matmul(
                    pk.ap()[:, 0:256], d[:, w : w + 128], hh_t[:, s], start=f, stop=False))
                p.raw("t", lambda h, d=wv, w=w0, s=hs_: h.matmul(
                    pk.ap()[:, 0:256], d[:, w : w + 128], hl_t[:, s], start=False, stop=False))
                fn = (lambda h, d=wv, w=w0, s=hs_, l=last: h.matmul(
                    pk.ap()[:, 0:256], d[:, 2048 + w : 2048 + w + 128], hh_t[:, s],
                    start=False, stop=l))
                if last:
                    mark = p.inc("t", fn)
                else:
                    p.raw("t", fn)
            wb.mark_user(mark)
            bufs["hs"].mark_user(mark)
            pkb.mark_ready(mark)
            return mark

        def fns3_optA(pk, wv):
            fns = []
            for dc in range(NCH):
                w0 = dc * 128
                hs_ = slice(dc * 256, dc * 256 + 256)
                first = dc == 0
                last = dc == NCH - 1
                fns.append(lambda h, d=wv, w=w0, s=hs_, f=first: h.matmul(
                    pk.ap()[:, 0:256], d[:, w : w + 128], hh_t[:, s], start=f, stop=False))
                fns.append(lambda h, d=wv, w=w0, s=hs_: h.matmul(
                    pk.ap()[:, 0:256], d[:, w : w + 128], hl_t[:, s], start=False, stop=False))
                fns.append(lambda h, d=wv, w=w0, s=hs_, l=last: h.matmul(
                    pk.ap()[:, 0:256], d[:, 2048 + w : 2048 + w + 128], hh_t[:, s],
                    start=False, stop=l))
            return fns

        p.mm_group([lambda h: h.matmul(pb[0].ap()[:, 0:32], onest.ap()[0:1, 0:128],
                                       rowt.ap()[0:1, 0:32], start=True, stop=True)],
                   reads=[bufs["consts"], bufs["r_sum"]], writes=[bufs["pb0"]])
        p.mm_group([lambda h: h.matmul(pb[1].ap()[:, 0:64], onest.ap()[0:1, 0:128],
                                       rowt.ap()[0:1, 0:64], start=True, stop=True)],
                   reads=[bufs["consts"], bufs["r_sum"]], writes=[bufs["pb1"]])
        if stages in ("all", "attn"):
            load_x(0, 0, xT)
            emit_ln_stats(xb.ap(), bufs["xb0"], par7=False)

        def emit_mod_window(w):
            tgt = pb[0] if w < 2 else pb[1]
            base = 0 if w < 2 else 32
            for dc in range(NCH):
                si = dc % 2
                sl = pA if si == 0 else pB
                slb = bufs["pA" if si == 0 else "pB"]
                ld = p.op("q", lambda h, d=sl, r0=dc * 128, c0=w * 2048:
                          h.dma_start(out=d[:, 0:2048], in_=wmod[r0 : r0 + 128, c0 : c0 + 2048]),
                          writes=[slb], sem=f"qm{si}")
                fns = []
                for jl in range(16):
                    ja = w * 16 + jl - base
                    fns.append(lambda h, d=sl, jl_=jl, ja_=ja, dc_=dc, t_=tgt:
                               h.matmul(t_.ap()[:, ja_ : ja_ + 1], d[:, jl_ * 128 : jl_ * 128 + 128],
                                        cst.ap()[:, dc_ : dc_ + 1], start=False, stop=False,
                                        skip_group_check=True))
                p.mm_group(fns, reads=[slb, bufs["cst"]], writes=[], extra=[ld])

        emit_mod_window(0)
        emit_mod_window(1)
        bufs["pb0"].mark_ready(("t", p.cnt["t"]))
        p.op("v", lambda h: h.tensor_copy(out=modct.ap()[:, 0:32], in_=pb[0].ap()[:, 0:32]),
             reads=[bufs["pb0"]], writes=[bufs["modct"]])
        p.op("s", lambda h: h.add(out=capc.ap()[:], in_=modct.ap()[:, 16:32], add=1.0),
             reads=[bufs["modct"]], writes=[bufs["modpost"]])

        wq_qbufs = []

        def emit_quarter(qi):
            for j in range(qi * 12, qi * 12 + 12):
                emit_split_slab(wq, j, wq_s, stsem=f"gq{qi * 2 + j % 2}")
            qb = Buf()
            qb.ready = [(f"gq{qi * 2}", p.cnt[f"gq{qi * 2}"]),
                        (f"gq{qi * 2 + 1}", p.cnt[f"gq{qi * 2 + 1}"])]
            wq_qbufs.append(qb)

        emit_quarter(0)
        hmarks0 = (emit_ln_apply(xb.ap(), bufs["xb0"], capc, 0)
                   if stages in ("all", "attn") else None)
        for w_, qi_ in ((2, 1), (3, 2), (4, 3)):
            emit_mod_window(w_)
            emit_quarter(qi_)
        emit_mod_window(5)
        bufs["pb1"].mark_ready(("t", p.cnt["t"]))
        bufs["wqS"].ready = [(f"gq{i}", p.cnt[f"gq{i}"]) for i in range(8)]
        for j in range(16):
            emit_split_slab(wg, j, wg_s, stsem=f"gw{j % 2}")
        bufs["wgS"].ready = [("gw0", p.cnt["gw0"]), ("gw1", p.cnt["gw1"])]
        for j in range(16):
            emit_split_slab(wo, j, wo_s, stsem=f"gw{2 + j % 2}")
        bufs["woS"].ready = [("gw2", p.cnt["gw2"]), ("gw3", p.cnt["gw3"])]

        p.op("v", lambda h: h.tensor_copy(out=modct.ap()[:, 32:96], in_=pb[1].ap()[:, 0:64]),
             reads=[bufs["pb1"]], writes=[bufs["modctB"]])
        p.op("s", lambda h: h.add(out=cbpc.ap()[:], in_=modct.ap()[:, 64:80], add=1.0),
             reads=[bufs["modctB"]], writes=[bufs["modpostB"]])
        for (g_src, g_dst, g_tmp) in ((32, gac, tmga), (80, gbc, tmgb)):
            p.op("v", lambda h, s0=g_src, t=g_tmp: h.tensor_scalar(
                out=t.ap()[:], in0=modct.ap()[:, s0 : s0 + 16], scalar1=0.0, scalar2=None,
                op0=ALU.is_ge), reads=[bufs["modctB"]], writes=[bufs["modpostB"]])
            p.op("v", lambda h, t=g_tmp: h.tensor_scalar(
                out=t.ap()[:], in0=t.ap()[:], scalar1=2.0, scalar2=-1.0,
                op0=ALU.mult, op1=ALU.add), reads=[bufs["modpostB"]], writes=[bufs["modpostB"]])
            p.op("v", lambda h, s0=g_src, d=g_dst: h.tensor_scalar(
                out=d.ap()[:], in0=modct.ap()[:, s0 : s0 + 16], scalar1=-1.0, scalar2=None,
                op0=ALU.mult), reads=[bufs["modctB"]], writes=[bufs["modpostB"]])
            p.op("v", lambda h, s0=g_src, d=g_dst: h.tensor_max(
                out=d.ap()[:], in0=d.ap()[:], in1=modct.ap()[:, s0 : s0 + 16]),
                reads=[bufs["modctB"], bufs["modpostB"]], writes=[bufs["modpostB"]])
            p.op("v", lambda h, d=g_dst: h.tensor_scalar(
                out=d.ap()[:], in0=d.ap()[:], scalar1=0.25, scalar2=None, op0=ALU.max),
                reads=[bufs["modpostB"]], writes=[bufs["modpostB"]])
            p.op("v", lambda h, d=g_dst, t=g_tmp: h.tensor_mul(
                out=d.ap()[:], in0=d.ap()[:], in1=t.ap()[:]),
                reads=[bufs["modpostB"]], writes=[bufs["modpostB"]])
        p.op("v", lambda h: h.tensor_scalar(out=gbc.ap()[:], in0=gbc.ap()[:],
                                            scalar1=1.0 / (WS * WS), scalar2=None, op0=ALU.mult),
             reads=[bufs["modpostB"]], writes=[bufs["modpostB"]])

        for b_ in (range(NBLK) if stages in ("all", "attn") else range(0)):
            t0 = b_ * BLK
            par = b_ % 2
            xcur = xb if par == 0 else xbB
            xcur_ap = xcur.ap()
            xcurb = bufs[f"xb{par}"]
            hmarks = hmarks0 if b_ == 0 else emit_ln_apply(xcur_ap, xcurb, capc, 0)

            for oc48 in range(48):
                wsb = wq_qbufs[oc48 // 12] if b_ == 0 else bufs["wqS"]
                wv, wb, lds = load_oc(oc48 % 4, wq_s, oc48, wsb)
                if oc48 < 32:
                    dest_ap = q_t if oc48 < 16 else k_t
                    dest_b = bufs["q"] if oc48 < 16 else bufs["k"]
                    oc = oc48 % 16
                    pk = pb[oc % 2]
                    pkb = bufs[f"pb{oc % 2}"]
                    if oc48 == 0:
                        mm_group_tracked(pk, pkb, wv, wb, lds, hmarks)
                    else:
                        p.mm_group(fns3_optA(pk, wv), reads=[wb, bufs["hs"]], writes=[pkb], extra=lds)
                    p.op("s", lambda h, da=dest_ap, o=oc, pk_=pk: h.activation(
                        out=da[:, o * 256 : (o + 1) * 256], in_=pk_.ap()[:, 0:256],
                        func=AF.Silu, scale=IWS),
                        reads=[pkb], writes=[dest_b])
                else:
                    hd = oc48 - 32
                    for th in range(2):
                        pk = pb[th]
                        pkb = bufs[f"pb{th}"]
                        fns = []
                        for dc in range(NCH):
                            h0 = dc * 256 + th * 128
                            first = dc == 0
                            last = dc == NCH - 1
                            fns.append(lambda h, hb_=h0, w=dc * 128, d=wv, f=first, pk_=pk: h.matmul(
                                pk_.ap()[:, 0:128], hh_t[:, hb_ : hb_ + 128], d[:, w : w + 128],
                                start=f, stop=False))
                            fns.append(lambda h, hb_=h0, w=dc * 128, d=wv, pk_=pk: h.matmul(
                                pk_.ap()[:, 0:128], hl_t[:, hb_ : hb_ + 128], d[:, w : w + 128],
                                start=False, stop=False))
                            fns.append(lambda h, hb_=h0, w=2048 + dc * 128, d=wv, l=last, pk_=pk: h.matmul(
                                pk_.ap()[:, 0:128], hh_t[:, hb_ : hb_ + 128], d[:, w : w + 128],
                                start=False, stop=l))
                        p.mm_group(fns, reads=[wb, bufs["hs"]], writes=[pkb], extra=lds)
                        vdst = vt0 if th == 0 else vt1
                        vdb = bufs["vt0" if th == 0 else "vt1"]
                        p.op("s", lambda h, vd=vdst, hd_=hd, pk_=pk: h.activation(
                            out=vd[:, hd_ * 128 : hd_ * 128 + 128],
                            in_=pk_.ap()[:, 0:128], func=AF.Silu, scale=IWS),
                            reads=[pkb], writes=[vdb])

            for oc in range(16):
                wv, wb, lds = load_oc(oc % 4, wg_s, oc, bufs["wgS"])
                pk = pb[oc % 2]
                pkb = bufs[f"pb{oc % 2}"]
                p.mm_group(fns3_optA(pk, wv), reads=[wb, bufs["hs"]], writes=[pkb], extra=lds)
                p.op("s", lambda h, o=oc, pk_=pk: h.activation(
                    out=gate_t[:, o * 256 : (o + 1) * 256], in_=pk_.ap()[:, 0:256],
                    func=AF.Sigmoid, scale=IWS), reads=[pkb], writes=[bufs["gate"]])

            if b_ + 1 < NBLK:
                load_x((b_ + 1) % 2, b_ + 1, xT)
                nxt = xb if (b_ + 1) % 2 == 0 else xbB
                emit_ln_stats(nxt.ap(), bufs[f"xb{(b_ + 1) % 2}"], par7=False)

            for h_ in range(NH):
                so = slot_off[h_ % 2]
                slotb = bufs[f"slot{h_ % 2}"]
                sqdec = ar3.ap()[:, so + 0 : so + 256]
                sdd0 = ar3.ap()[:, so + 256 : so + 512]
                sdd1 = ar3.ap()[:, so + 512 : so + 768]
                sqkd0 = sdd0
                sqkd1 = sdd1
                sqs = ar3.ap()[:, so + 768 : so + 1024]
                sk2 = ar3.ap()[:, so + 1024 : so + 1280]
                kvh = kv_t[:, h_ * HD : (h_ + 1) * HD]
                kvb = bufs[f"kv{h_}"]
                qch = q_t[:, h_ * 256 : (h_ + 1) * 256]
                kch = k_t[:, h_ * 256 : (h_ + 1) * 256]

                p.op("s", lambda h, d=sqdec, i=h_: h.activation(out=d, in_=rampt.ap()[:],
                                                                func=AF.Exp,
                                                                scale=negst.ap()[:, i : i + 1]),
                     reads=[bufs["consts"], bufs["decays"]], writes=[slotb])
                p.op("s", lambda h, d=sdd0, i=h_: h.activation(out=d, in_=idx0t.ap()[:],
                                                               func=AF.Exp,
                                                               scale=negst.ap()[:, i : i + 1]),
                     reads=[bufs["consts"], bufs["decays"]], writes=[slotb])
                p.op("s", lambda h, d=sdd1, i=h_: h.activation(out=d, in_=idx1t.ap()[:],
                                                               func=AF.Exp,
                                                               scale=negst.ap()[:, i : i + 1]),
                     reads=[bufs["consts"], bufs["decays"]], writes=[slotb])
                p.mm_group([lambda h, kc=kch, qc=qch: h.matmul(pb[2].ap()[:, 0:256],
                                                               kc[:, 0:128], qc,
                                                               start=True, stop=True)],
                           reads=[bufs["k"], bufs["q"]], writes=[bufs["pb2"]])
                p.mm_group([lambda h, kc=kch, qc=qch: h.matmul(pb[3].ap()[:, 0:256],
                                                               kc[:, 128:256], qc,
                                                               start=True, stop=True)],
                           reads=[bufs["k"], bufs["q"]], writes=[bufs["pb3"]])
                p.op("v", lambda h, d=sqkd0, dd=sdd0: h.tensor_mul(out=d, in0=pb[2].ap()[:, 0:256], in1=dd),
                     reads=[bufs["pb2"], slotb], writes=[slotb])
                p.op("v", lambda h, d=sqkd1, dd=sdd1: h.tensor_mul(out=d, in0=pb[3].ap()[:, 0:256], in1=dd),
                     reads=[bufs["pb3"], slotb], writes=[slotb])
                p.op("v", lambda h, d=sqs, qc=qch, dec=sqdec: h.tensor_mul(out=d, in0=qc, in1=dec),
                     reads=[bufs["q"], slotb], writes=[slotb])
                p.mm_group([
                    lambda h, kv_=kvh, d=sqs: h.matmul(pb[4].ap()[:, 0:256], kv_, d,
                                                       start=True, stop=False),
                    lambda h, i=h_, d=sqkd0: h.matmul(pb[4].ap()[:, 0:256],
                                                      vt0[:, i * HD : (i + 1) * HD], d,
                                                      start=False, stop=False),
                    lambda h, i=h_, d=sqkd1: h.matmul(pb[4].ap()[:, 0:256],
                                                      vt1[:, i * HD : (i + 1) * HD], d,
                                                      start=False, stop=True),
                ], reads=[kvb, slotb, bufs["vt0"], bufs["vt1"]], writes=[bufs["pb4"]])
                p.mm_group([lambda h, kc=kch: h.transpose(pb[5].ap()[:, 0:128], kc[:, 0:128],
                                                          identt.ap()[:])],
                           reads=[bufs["k"], bufs["consts"]], writes=[bufs["pb5"]])
                p.op("s", lambda h, d=sk2, i=h_: h.activation(out=d[:, 0:128], in_=pb[5].ap()[:, 0:128],
                                                              func=AF.Identity,
                                                              scale=kdect.ap()[:, 2 * i : 2 * i + 1]),
                     reads=[bufs["pb5"], bufs["decays"]], writes=[slotb])
                p.mm_group([lambda h, kc=kch: h.transpose(pb[5].ap()[:, 128:256], kc[:, 128:256],
                                                          identt.ap()[:])],
                           reads=[bufs["k"], bufs["consts"]], writes=[bufs["pb5"]])
                p.op("s", lambda h, d=sk2, i=h_: h.activation(out=d[:, 128:256], in_=pb[5].ap()[:, 128:256],
                                                              func=AF.Identity,
                                                              scale=kdect.ap()[:, 2 * i + 1 : 2 * i + 2]),
                     reads=[bufs["pb5"], bufs["decays"]], writes=[slotb])
                p.mm_group([
                    lambda h, d=sk2, i=h_: h.matmul(pb[5].ap()[:, 256:384], d[:, 0:128],
                                                    vt0[:, i * HD : (i + 1) * HD],
                                                    start=True, stop=False),
                    lambda h, d=sk2, i=h_: h.matmul(pb[5].ap()[:, 256:384], d[:, 128:256],
                                                    vt1[:, i * HD : (i + 1) * HD],
                                                    start=False, stop=True),
                ], reads=[slotb, bufs["vt0"], bufs["vt1"]], writes=[bufs["pb5"]])
                p.op("v", lambda h, kv_=kvh, i=h_: h.scalar_tensor_tensor(
                    out=kv_, in0=kv_, scalar=bdct.ap()[:, i : i + 1], in1=pb[5].ap()[:, 256:384],
                    op0=ALU.mult, op1=ALU.add),
                    reads=[kvb, bufs["pb5"], bufs["decays"]], writes=[kvb])
                m_osq = p.op("s", lambda h, d=sqs: h.activation(out=d, in_=pb[4].ap()[:, 0:256],
                                                                func=AF.Square),
                             reads=[bufs["pb4"]], writes=[slotb])
                p.mm_group([lambda h, d=sqs, i=h_: h.matmul(pb[7].ap()[0:1, 0:256], onest.ap()[:, 0:1],
                                                            d, start=(i == 0), stop=(i == NH - 1))],
                           reads=[slotb, bufs["consts"]],
                           writes=([bufs["pb7"]] if h_ in (0, NH - 1) else []))
                p.op("v", lambda h, i=h_: h.scalar_tensor_tensor(
                    out=hz.ap()[:, i * 256 : (i + 1) * 256],
                    in0=gate_t[:, i * 256 : (i + 1) * 256],
                    scalar=normwt.ap()[:, i : i + 1], in1=pb[4].ap()[:, 0:256],
                    op0=ALU.mult, op1=ALU.mult),
                    reads=[bufs["pb4"], bufs["gate"], bufs["consts"]], writes=[bufs["hz"]],
                    extra=[m_osq])
                p.op("v", lambda h, i=h_: h.tensor_copy(out=hh_t[:, i * 256 : (i + 1) * 256],
                                                        in_=hz.ap()[:, i * 256 : (i + 1) * 256]),
                     reads=[bufs["hz"]], writes=[bufs["hs"]])
                p.op("v", lambda h, i=h_: h.tensor_sub(out=hl_t[:, i * 256 : (i + 1) * 256],
                                                       in0=hz.ap()[:, i * 256 : (i + 1) * 256],
                                                       in1=hh_t[:, i * 256 : (i + 1) * 256]),
                     reads=[bufs["hz"], bufs["hs"]], writes=[bufs["hs"]])

            p.op("v", lambda h: h.tensor_copy(out=row("r_rsq"), in_=pb[7].ap()[0:1, 0:256]),
                 reads=[bufs["pb7"]], writes=[bufs["r_rsq"]])
            p.op("s", lambda h: h.activation(out=row("r_rtmp"), in_=row("r_rsq"), func=AF.Sqrt,
                                             scale=WS * WS / float(D), bias=eps6x.ap()[0:1, :]),
                 reads=[bufs["r_rsq"], bufs["consts"]], writes=[bufs["r_rtmp"]])
            p.op("v", lambda h: h.reciprocal(out=row("r_rrstd"), in_=row("r_rtmp")),
                 reads=[bufs["r_rtmp"]], writes=[bufs["r_rrstd"]])
            p.mm_group([lambda h: h.matmul(pb[7].ap()[:, 256:512], onest.ap()[0:1, 0:128],
                                           row("r_rrstd"), start=True, stop=True)],
                       reads=[bufs["r_rrstd"], bufs["consts"]], writes=[bufs["pb7"]])
            p.op("v", lambda h: h.tensor_copy(out=rmsR.ap()[:], in_=pb[7].ap()[:, 256:512]),
                 reads=[bufs["pb7"]], writes=[bufs["rmsR"]])

            for fc in range(16):
                wv, wb, lds = load_oc(fc % 4, wo_s, fc, bufs["woS"])
                pk = pb[fc % 2]
                pkb = bufs[f"pb{fc % 2}"]
                p.mm_group(fns3_optA(pk, wv), reads=[wb, bufs["hs"]], writes=[pkb], extra=lds)
                p.op("v", lambda h, f=fc, pk_=pk: h.tensor_mul(
                    out=scr(f), in0=pk_.ap()[:, 0:256], in1=rmsR.ap()[:]),
                    reads=[pkb, bufs["rmsR"]], writes=[scrb(fc)])
                p.op("v", lambda h, f=fc, xa=xcur_ap: h.scalar_tensor_tensor(
                    out=xa[:, f * 256 : (f + 1) * 256], in0=scr(f),
                    scalar=gac.ap()[:, f : f + 1],
                    in1=xa[:, f * 256 : (f + 1) * 256],
                    op0=ALU.mult, op1=ALU.add),
                    reads=[scrb(fc), bufs["modpostB"], xcurb], writes=[xcurb])
            p.op("g", lambda h, t=t0, xa=xcur_ap: h.dma_start(
                out=x2d.rearrange("(c p) x -> p c x", p=128)[:, 0:8, t : t + BLK],
                in_=xa.rearrange("p (c w) -> p c w", c=NCH)[:, 0:8, :]),
                reads=[xcurb], writes=[bufs[f"x2_{b_}"]])
            p.op("g", lambda h, t=t0, xa=xcur_ap: h.dma_start(
                out=x2d.rearrange("(c p) x -> p c x", p=128)[:, 8:16, t : t + BLK],
                in_=xa.rearrange("p (c w) -> p c w", c=NCH)[:, 8:16, :]),
                reads=[xcurb], writes=[bufs[f"x2_{b_}"]])

        p.barrier()
        if stages in ("all", "attn", "fold"):
            for uc in range(HCH):
                foldA_iter(uc)
                emit_split_slab(w1, uc, w1_s)
            for vg in range(64):
                foldB_iter(vg)
                emit_split_slab(w3, vg, w3_s)
            bufs["w1S"].ready = [("gs0", p.cnt["gs0"]), ("gs1", p.cnt["gs1"])]
            bufs["w3S"].ready = [("gs0", p.cnt["gs0"]), ("gs1", p.cnt["gs1"])]
            bufs["w2fS"].ready = [("gs2", p.cnt["gs2"]), ("gs3", p.cnt["gs3"])]

        p.barrier()

        if stages == "all":
            load_x(0, 0, x2d, x2buf=bufs["x2_0"])
            emit_ln_stats(xb.ap(), bufs["xb0"], par7=True)
        hm_next = None
        for st in (range(NBLK) if stages == "all" else range(0)):
            t0 = st * BLK
            par = st % 2
            xcur = xb if par == 0 else xbB
            xcur_ap = xcur.ap()
            xcurb = bufs[f"xb{par}"]
            hmarks = hm_next if st > 0 else emit_ln_apply(
                xcur_ap, xcurb, cbpc, 48, mbufs=[bufs["modpostB"], bufs["modctB"]])

            for half in range(2):
                for oc in range(32):
                    for wsel in range(2):
                        srcs = w1_s if wsel == 0 else w3_s
                        wsb = bufs["w1S" if wsel == 0 else "w3S"]
                        slot = (oc * 2 + wsel) % 4
                        wv, wb, lds = load_oc(slot, srcs, half * 32 + oc, wsb)
                        pk = pb[wsel * 2 + (oc % 2)]
                        pkb = bufs[f"pb{wsel * 2 + (oc % 2)}"]
                        if half == 0 and oc == 0 and wsel == 0:
                            mm_group_tracked(pk, pkb, wv, wb, lds, hmarks)
                        else:
                            p.mm_group(fns3_optA(pk, wv), reads=[wb, bufs["hs"]], writes=[pkb], extra=lds)
                        if wsel == 0:
                            p.op("s", lambda h, o=oc, pk_=pk: h.activation(
                                out=A_t[:, o * 256 : (o + 1) * 256], in_=pk_.ap()[:, 0:256],
                                func=AF.Silu, scale=IWS), reads=[pkb], writes=[bufs["A"]])
                        else:
                            p.op("v", lambda h, o=oc, pk_=pk: h.tensor_copy(
                                out=B_t[:, o * 256 : (o + 1) * 256], in_=pk_.ap()[:, 0:256]),
                                reads=[pkb], writes=[bufs["Bf"]])
                    p.op("v", lambda h, o=oc: h.tensor_mul(
                        out=A_t[:, o * 256 : (o + 1) * 256],
                        in0=A_t[:, o * 256 : (o + 1) * 256],
                        in1=B_t[:, o * 256 : (o + 1) * 256]),
                        reads=[bufs["A"], bufs["Bf"]], writes=[bufs["A"]])

                if half == 1 and st + 1 < NBLK:
                    load_x((st + 1) % 2, st + 1, x2d, x2buf=bufs[f"x2_{st + 1}"])
                    nxt = xb if (st + 1) % 2 == 0 else xbB
                    emit_ln_stats(nxt.ap(), bufs[f"xb{(st + 1) % 2}"], par7=True)
                    hm_next = emit_ln_apply(nxt.ap(), bufs[f"xb{(st + 1) % 2}"], cbpc, 48,
                                            mbufs=[bufs["modpostB"], bufs["modctB"]])

                aspg = []
                for g4 in range(8):
                    s4 = slice(g4 * 1024, (g4 + 1) * 1024)
                    p.op("v", lambda h, s=s4: h.tensor_copy(out=Ah_t[:, s], in_=A_t[:, s]),
                         reads=[bufs["A"]], writes=[bufs["Bf"]])
                    m_ = p.op("v", lambda h, s=s4: h.tensor_sub(out=Al_t[:, s], in0=A_t[:, s],
                                                                in1=Ah_t[:, s]),
                              reads=[bufs["A"]], writes=[bufs["Asp"], bufs["Bf"]])
                    aspg.append(m_)

                for wave in range(2):
                    wavebanks = [bufs[f"pb{i}"] for i in range(8)]
                    p.deps("t", reads=[bufs["w2fS"]], writes=wavebanks)
                    for uc in range(32):
                        slot = uc % 4
                        off = (slot % 2) * 4096
                        shb = wsl16[slot // 2].ap()[:, off : off + 2048]
                        shbb = bufs[SLOTB[slot]]
                        r0 = half * 4096 + uc * 128
                        ldm2 = p.op("s", lambda h, d=shb, r=r0, w=wave: h.dma_start(
                            out=d.rearrange("p (pl c) -> p pl c", pl=2),
                            in_=w2f_s.rearrange("r (pl c) -> r pl c", pl=2)[r : r + 128, :, w * 1024 : w * 1024 + 1024]),
                            reads=[bufs["w2fS"]], writes=[shbb], sem=f"qw{slot}")
                        p._wait("t", ldm2)
                        p._wait("t", aspg[uc // 4])
                        us_ = slice(uc * 256, (uc + 1) * 256)
                        for fcl in range(8):
                            w0 = fcl * 128
                            p.raw("t", lambda h, d=shb, w=w0, s=us_, f=fcl, u=uc: h.matmul(
                                pb[f].ap()[:, 0:256], d[:, w : w + 128], Ah_t[:, s],
                                start=(u == 0), stop=False))
                            p.raw("t", lambda h, d=shb, w=w0, s=us_, f=fcl: h.matmul(
                                pb[f].ap()[:, 0:256], d[:, w : w + 128], Al_t[:, s],
                                start=False, stop=False))
                            fn = (lambda h, d=shb, w=w0, s=us_, f=fcl, u=uc: h.matmul(
                                pb[f].ap()[:, 0:256], d[:, 1024 + w : 1024 + w + 128], Ah_t[:, s],
                                start=False, stop=(u == 31)))
                            if fcl == 7:
                                m = p.inc("t", fn)
                                shbb.mark_user(m)
                                if uc == 31:
                                    lastmark = m
                            else:
                                p.raw("t", fn)
                        if uc == 31:
                            for i in range(8):
                                bufs[f"pb{i}"].mark_ready(lastmark)
                    bufs["Asp"].mark_user(lastmark)
                    for fcl in range(8):
                        fc = wave * 8 + fcl
                        if half == 0:
                            p.op("v", lambda h, f=fc, fl=fcl: h.tensor_copy(
                                out=ya_t[:, f * 256 : (f + 1) * 256], in_=pb[fl].ap()[:, 0:256]),
                                reads=[bufs[f"pb{fcl}"]], writes=[bufs["ya"]])
                        else:
                            p.op("v", lambda h, f=fc, fl=fcl: h.tensor_add(
                                out=ya_t[:, f * 256 : (f + 1) * 256],
                                in0=ya_t[:, f * 256 : (f + 1) * 256], in1=pb[fl].ap()[:, 0:256]),
                                reads=[bufs[f"pb{fcl}"], bufs["ya"]], writes=[bufs["ya"]])

            for fc in range(NCH):
                p.op("v", lambda h, f=fc, xa=xcur_ap: h.scalar_tensor_tensor(
                    out=ya_t[:, f * 256 : (f + 1) * 256], in0=ya_t[:, f * 256 : (f + 1) * 256],
                    scalar=gbc.ap()[:, f : f + 1], in1=xa[:, f * 256 : (f + 1) * 256],
                    op0=ALU.mult, op1=ALU.add),
                    reads=[bufs["ya"], bufs["modpostB"], xcurb], writes=[bufs["ya"]])
            p.op("g", lambda h, t=t0: h.dma_start(
                out=yT.rearrange("(c p) x -> p c x", p=128)[:, 0:8, t : t + BLK],
                in_=ya_t.rearrange("p (c w) -> p c w", c=NCH)[:, 0:8, :]),
                reads=[bufs["ya"]], writes=[])
            p.op("g", lambda h, t=t0: h.dma_start(
                out=yT.rearrange("(c p) x -> p c x", p=128)[:, 8:16, t : t + BLK],
                in_=ya_t.rearrange("p (c w) -> p c w", c=NCH)[:, 8:16, :]),
                reads=[bufs["ya"]], writes=[])


        if stages != "all":
            p.op("g", lambda h: h.dma_start(out=yT[0:128, 0:96], in_=modct.ap()[:]),
                 reads=[bufs["modct"], bufs["modpost"], bufs["modctB"], bufs["modpostB"]], writes=[])
        gfin = p.cnt["g"]
        p.ops["g"].append(("w", "g", gfin))
        p.ops["g"].append(("w", "gs0", p.cnt["gs0"]))
        p.ops["g"].append(("w", "gs1", p.cnt["gs1"]))
        p.ops["g"].append(("w", "gs2", p.cnt["gs2"]))
        p.ops["g"].append(("w", "gs3", p.cnt["gs3"]))

        def run(e, h):
            for item in p.ops[e]:
                if item[0] == "w":
                    h.wait_ge(sems[item[1]], item[2])
                else:
                    inst = item[1](h)
                    if item[2] is not None:
                        inst.then_inc(sems[item[2]], 16 if item[2] in DMA_SEMS else 1)

        @block.tensor
        def _(tensor):
            run("t", tensor)

        @block.vector
        def _(vector):
            run("v", vector)

        @block.scalar
        def _(scalar):
            run("s", scalar)

        @block.sync
        def _(sync):
            run("q", sync)

        @block.gpsimd
        def _(gpsimd):
            run("g", gpsimd)

    return nc


def _hadamard(n):
    h = np.array([[1.0]], dtype=np.float64)
    while h.shape[0] < n:
        h = np.block([[h, h], [h, -h]])
    return h


def make_inputs(x, slope, w_qkv, w_gate, w_out, norm_w, W1, W2, W3, c, wmod, T):
    f32 = np.float32
    x = np.asarray(x, f32)
    nb = x.shape[0]
    wqkvT = np.ascontiguousarray(np.asarray(w_qkv, f32).T)
    o = np.arange(3 * D)
    h_, r, j = o // 384, (o % 384) // 128, o % 128
    new = r * D + h_ * 128 + j
    src = np.empty(3 * D, np.int64)
    src[new] = o
    wq = np.ascontiguousarray(wqkvT[:, src])
    wg = np.ascontiguousarray(np.asarray(w_gate, f32).T)
    wo = np.ascontiguousarray(np.asarray(w_out, f32).T)
    w1 = np.ascontiguousarray(np.asarray(W1, f32).T)
    w3 = np.ascontiguousarray(np.asarray(W3, f32).T)
    w2t = np.ascontiguousarray(np.asarray(W2, f32).T)
    wmod = np.ascontiguousarray(np.asarray(wmod, f32))
    slope = np.asarray(slope, f32).reshape(NH)
    scols = np.ascontiguousarray(np.broadcast_to(slope[None, :], (128, NH)))
    normwc = np.ascontiguousarray(np.asarray(norm_w, f32).reshape(NCH, 128).T)
    ident = np.eye(128, dtype=f32)
    ramp = np.ascontiguousarray(np.broadcast_to(np.arange(1, BLK + 1, dtype=f32)[None, :], (128, BLK)))
    mm, nn = np.meshgrid(np.arange(BLK, dtype=f32), np.arange(128, dtype=f32))
    idx0 = np.where(mm >= nn, mm - nn, 1e9).astype(f32)
    idx1 = np.where(mm >= nn + 128, mm - nn - 128, 1e9).astype(f32)
    kcol = np.ascontiguousarray(np.stack([255.0 - np.arange(128, dtype=f32),
                                          127.0 - np.arange(128, dtype=f32)], axis=1))
    h128 = _hadamard(128).astype(f32)
    h64 = _hadamard(64) / np.sqrt(8192.0)
    h64d = np.zeros((128, 128), np.float64)
    h64d[0:64, 0:64] = h64
    h64d[64:128, 64:128] = h64
    h64d = h64d.astype(f32)
    shared = dict(wq=wq, wg=wg, wo=wo, w1=w1, w3=w3, w2t=w2t, wmod=wmod, scols=scols,
                  normw=normwc, identc=ident, rampc=ramp, idx0c=idx0, idx1c=idx1,
                  kcolc=kcol, h128c=h128, h64dc=h64d)
    c = np.asarray(c, f32)
    in_maps = []
    for b_ in range(nb):
        m = dict(shared)
        m["xT"] = np.ascontiguousarray(x[b_, :T, :].T)
        m["cT"] = np.ascontiguousarray(c[b_].reshape(NCH, 128).T)
        in_maps.append(m)
    return in_maps


def kernel(L0_attn_hidden_states, L0_attn_slope_rate, L0_attn_w_qkv, L0_attn_w_gate,
           L0_attn_w_out, L0_attn_norm_w, L0_ffn_W1, L0_ffn_W2, L0_ffn_W3, L0_c, L0_wmod):
    T = 2048
    in_maps = make_inputs(L0_attn_hidden_states, L0_attn_slope_rate, L0_attn_w_qkv,
                          L0_attn_w_gate, L0_attn_w_out, L0_attn_norm_w,
                          L0_ffn_W1, L0_ffn_W2, L0_ffn_W3, L0_c, L0_wmod, T)
    nc = build(T)
    res = run_bass_kernel_spmd(nc, in_maps, core_ids=list(range(8)))
    out = np.stack([np.ascontiguousarray(r["yT"].T) for r in res.results], axis=0)
    return out.astype(np.float32)


# version 19
# speedup vs baseline: 1.0897x; 1.0897x over previous
import contextlib
import numpy as np

import concourse.bass as bass
import concourse.mybir as mybir
from concourse.bass_utils import run_bass_kernel_spmd

F32 = mybir.dt.float32
F16 = mybir.dt.float16
AF = mybir.ActivationFunctionType
ALU = mybir.AluOpType

D = 2048
NCH = 16
NH = 16
HD = 128
BLK = 256
HF = 8192
HCH = 64
WS = 64.0
IWS = 1.0 / WS

ENGS = ("t", "v", "s", "q", "g")
SEMS = ("t", "v", "s", "qc", "qx", "qx1", "qw0", "qw1", "qw2", "qw3", "qu0", "qu1", "g", "gs0", "gs1", "gs2", "gs3", "gq0", "gq1", "gq2", "gq3", "gq4", "gq5", "gq6", "gq7", "gw0", "gw1", "gw2", "gw3", "qm0", "qm1")
DMA_SEMS = ("qc", "qx", "qx1", "qw0", "qw1", "qw2", "qw3", "qu0", "qu1", "g", "gs0", "gs1", "gs2", "gs3", "gq0", "gq1", "gq2", "gq3", "gq4", "gq5", "gq6", "gq7", "gw0", "gw1", "gw2", "gw3", "qm0", "qm1")


class Buf:
    __slots__ = ("ready", "users")

    def __init__(self):
        self.ready = []
        self.users = []

    def r(self):
        return list(self.ready)

    def w(self):
        return list(self.users) + list(self.ready)

    def mark_user(self, mark):
        self.users.append(mark)

    def mark_ready(self, mark):
        self.ready = [mark]
        self.users = []


class P:
    def __init__(self):
        self.ops = {e: [] for e in ENGS}
        self.cnt = {sm: 0 for sm in SEMS}
        self.hw = {e: {sm: 0 for sm in SEMS} for e in ENGS}

    def _wait(self, eng, dep):
        f, val = dep
        if val <= 0:
            return
        if eng == "t" and f == "t":
            return
        if self.hw[eng][f] >= val:
            return
        self.hw[eng][f] = val
        self.ops[eng].append(("w", f, val))

    def deps(self, eng, reads=(), writes=(), extra=()):
        acc = {}
        for b in reads:
            for f, v in b.r():
                acc[f] = max(acc.get(f, 0), v)
        for b in writes:
            for f, v in b.w():
                acc[f] = max(acc.get(f, 0), v)
        for f, v in extra:
            acc[f] = max(acc.get(f, 0), v)
        for f, v in acc.items():
            self._wait(eng, (f, v))

    def raw(self, eng, fn):
        self.ops[eng].append(("i", fn, None))

    def inc(self, eng, fn, sem=None):
        sm = sem if sem is not None else ("g" if eng == "g" else eng)
        assert not (eng == "q" and sem is None), "sync-queue ops must name a DMA sem"
        step = 16 if sm in DMA_SEMS else 1
        self.cnt[sm] += step
        self.ops[eng].append(("i", fn, sm))
        return (sm, self.cnt[sm])

    def op(self, eng, fn, reads=(), writes=(), extra=(), sem=None):
        self.deps(eng, reads, writes, extra)
        mark = self.inc(eng, fn, sem)
        for b in reads:
            b.mark_user(mark)
        for b in writes:
            b.mark_ready(mark)
        return mark

    def mm_group(self, fns, reads=(), writes=(), extra=()):
        self.deps("t", reads, writes, extra)
        for f in fns[:-1]:
            self.raw("t", f)
        mark = self.inc("t", fns[-1])
        for b in reads:
            b.mark_user(mark)
        for b in writes:
            b.mark_ready(mark)
        return mark

    def barrier(self):
        snap = dict(self.cnt)
        for e in ENGS:
            for sm in SEMS:
                self._wait(e, (sm, snap[sm]))


def build(T, stages="all"):
    NBLK = T // BLK
    nc = bass.Bass("TRN2", target_bir_lowering=False)

    xT = nc.dram_tensor("xT", [D, T], F32, kind="ExternalInput").ap()
    wq = nc.dram_tensor("wq", [D, 3 * D], F32, kind="ExternalInput").ap()
    wg = nc.dram_tensor("wg", [D, D], F32, kind="ExternalInput").ap()
    wo = nc.dram_tensor("wo", [D, D], F32, kind="ExternalInput").ap()
    w1 = nc.dram_tensor("w1", [D, HF], F32, kind="ExternalInput").ap()
    w3 = nc.dram_tensor("w3", [D, HF], F32, kind="ExternalInput").ap()
    w2t = nc.dram_tensor("w2t", [HF, D], F32, kind="ExternalInput").ap()
    wmod = nc.dram_tensor("wmod", [D, 6 * D], F32, kind="ExternalInput").ap()
    cT = nc.dram_tensor("cT", [128, NCH], F32, kind="ExternalInput").ap()
    scols = nc.dram_tensor("scols", [128, NH], F32, kind="ExternalInput").ap()
    normw = nc.dram_tensor("normw", [128, NCH], F32, kind="ExternalInput").ap()
    identc = nc.dram_tensor("identc", [128, 128], F32, kind="ExternalInput").ap()
    rampc = nc.dram_tensor("rampc", [128, BLK], F32, kind="ExternalInput").ap()
    idx0c = nc.dram_tensor("idx0c", [128, BLK], F32, kind="ExternalInput").ap()
    idx1c = nc.dram_tensor("idx1c", [128, BLK], F32, kind="ExternalInput").ap()
    kcolc = nc.dram_tensor("kcolc", [128, 2], F32, kind="ExternalInput").ap()
    h128c = nc.dram_tensor("h128c", [128, 128], F32, kind="ExternalInput").ap()
    h64dc = nc.dram_tensor("h64dc", [128, 128], F32, kind="ExternalInput").ap()
    yT = nc.dram_tensor("yT", [D, T], F32, kind="ExternalOutput").ap()
    x2d = nc.dram_tensor("x2d", [D, T], F32, kind="Internal").ap()
    tAd = nc.dram_tensor("tAd", [HF, D], F32, kind="Internal").ap()
    w2fd = nc.dram_tensor("w2fd", [HF, D], F32, kind="Internal").ap()
    wq_s = nc.dram_tensor("wq_s", [48, 128, 4096], F16, kind="Internal").ap()
    wg_s = nc.dram_tensor("wg_s", [16, 128, 4096], F16, kind="Internal").ap()
    wo_s = nc.dram_tensor("wo_s", [16, 128, 4096], F16, kind="Internal").ap()
    w1_s = nc.dram_tensor("w1_s", [64, 128, 4096], F16, kind="Internal").ap()
    w3_s = nc.dram_tensor("w3_s", [64, 128, 4096], F16, kind="Internal").ap()
    w2f_s = nc.dram_tensor("w2f_s", [HF, 2 * D], F16, kind="Internal").ap()

    p = P()

    with contextlib.ExitStack() as es:
        E = es.enter_context
        identt = E(nc.sbuf_tensor("identt", [128, 128], F32))
        onest = E(nc.sbuf_tensor("onest", [128, 128], F32))
        rampt = E(nc.sbuf_tensor("rampt", [128, BLK], F32))
        idx0t = E(nc.sbuf_tensor("idx0t", [128, BLK], F32))
        idx1t = E(nc.sbuf_tensor("idx1t", [128, BLK], F32))
        kcolt = E(nc.sbuf_tensor("kcolt", [128, 2], F32))
        h128t = E(nc.sbuf_tensor("h128t", [128, 128], F32))
        h64dt = E(nc.sbuf_tensor("h64dt", [128, 128], F32))
        scolt = E(nc.sbuf_tensor("scolt", [128, NH], F32))
        negst = E(nc.sbuf_tensor("negst", [128, NH], F32))
        kdect = E(nc.sbuf_tensor("kdect", [128, 2 * NH], F32))
        bdct = E(nc.sbuf_tensor("bdct", [128, NH], F32))
        cst = E(nc.sbuf_tensor("cst", [128, NCH], F32))
        modct = E(nc.sbuf_tensor("modct", [128, 96], F32))
        capc = E(nc.sbuf_tensor("capc", [128, NCH], F32))
        cbpc = E(nc.sbuf_tensor("cbpc", [128, NCH], F32))
        gac = E(nc.sbuf_tensor("gac", [128, NCH], F32))
        gbc = E(nc.sbuf_tensor("gbc", [128, NCH], F32))
        tmga = E(nc.sbuf_tensor("tmga", [128, NCH], F32))
        tmgb = E(nc.sbuf_tensor("tmgb", [128, NCH], F32))
        normwt = E(nc.sbuf_tensor("normwt", [128, NCH], F32))
        eps5c = E(nc.sbuf_tensor("eps5c", [128, 1], F32))
        eps6x = E(nc.sbuf_tensor("eps6x", [128, 1], F32))
        xb = E(nc.sbuf_tensor("xb", [128, NCH * BLK], F32))
        xbB = E(nc.sbuf_tensor("xbB", [128, NCH * BLK], F32))
        scrA = E(nc.sbuf_tensor("scrA", [128, 8 * BLK], F32))
        hz = E(nc.sbuf_tensor("hz", [128, NCH * BLK], F32))
        hsp = E(nc.sbuf_tensor("hsp", [128, 2 * NCH * BLK], F16))
        arQ = E(nc.sbuf_tensor("arQ", [128, 2 * NCH * BLK], F32))
        arK = E(nc.sbuf_tensor("arK", [128, 2 * NCH * BLK], F32))
        ar3 = E(nc.sbuf_tensor("ar3", [128, 4608], F32))
        rowt = E(nc.sbuf_tensor("rowt", [1, 1536], F32))
        rstdR = E(nc.sbuf_tensor("rstdR", [128, BLK], F32))
        mrR = E(nc.sbuf_tensor("mrR", [128, BLK], F32))
        rmsR = E(nc.sbuf_tensor("rmsR", [128, BLK], F32))
        wsl16 = [E(nc.sbuf_tensor(f"wsl16_{i}", [128, 2 * NCH * BLK], F16)) for i in range(2)]
        pb = [E(nc.psum_tensor(f"pb{i}", [128, 512], F32)) for i in range(8)]

        sems = {sm: E(nc.semaphore(f"sem_{sm}")) for sm in SEMS}
        block = E(nc.Block())

        q_t = arQ.ap()[:, 0 : NCH * BLK]
        gate_t = arQ.ap()[:, NCH * BLK : 2 * NCH * BLK]
        A_t = arQ.ap()
        pA = arQ.ap()[:, 0 : NCH * BLK]
        pB = arQ.ap()[:, NCH * BLK : 2 * NCH * BLK]
        pC = arK.ap()[:, 0 : NCH * BLK]
        pD = arK.ap()[:, NCH * BLK : 2 * NCH * BLK]
        k_t = arK.ap()[:, 0 : NCH * BLK]
        vt0 = arK.ap()[:, NCH * BLK : NCH * BLK + NH * HD]
        vt1 = arK.ap()[:, NCH * BLK + NH * HD : NCH * BLK + 2 * NH * HD]
        B_t = arK.ap()
        Ah_t = arK.ap()[:, 0 : NCH * BLK].bitcast(F16)
        Al_t = arK.ap()[:, NCH * BLK : 2 * NCH * BLK].bitcast(F16)
        hh_t = hsp.ap()[:, 0 : NCH * BLK]
        hl_t = hsp.ap()[:, NCH * BLK : 2 * NCH * BLK]
        kv_t = ar3.ap()[:, 0:2048]
        slot_off = (2048, 3328)
        ya_t = ar3.ap()[:, 0:4096]

        bufs = {}

        def mkbuf(*names):
            for n in names:
                bufs[n] = Buf()

        mkbuf("consts", "cst", "modct", "modpost", "modctB", "modpostB", "xb0", "xb1", "hz", "hs", "gate",
              "q", "k", "vt0", "vt1", "rstdR", "mrR", "rmsR",
              "r_sum", "r_sq", "r_a", "r_b", "r_rstd", "r_mr",
              "pA", "pB", "pC", "pD", "wslh0", "wslh1", "wsll0", "wsll1",
              "w2s0", "w2s1", "decays", "ya", "A", "Bf", "Asp")
        for i in range(8):
            mkbuf(f"pb{i}", f"scr{i}")
        for h in range(NH):
            mkbuf(f"kv{h}")
        for i in range(2):
            mkbuf(f"slot{i}")
        bufs["r_rsq"] = bufs["r_sq"]
        bufs["r_rtmp"] = bufs["r_b"]
        bufs["r_rrstd"] = bufs["r_rstd"]
        for b_ in range(NBLK):
            mkbuf(f"x2_{b_}")
        mkbuf("tAd", "w2fd")
        for w in ("wq", "wg", "wo", "w1", "w3", "w2f"):
            mkbuf(w + "S")

        ROW = {"r_sum": 0, "r_sq": 256, "r_a": 512, "r_b": 768, "r_rstd": 1024,
               "r_mr": 1280, "r_rsq": 256, "r_rtmp": 768, "r_rrstd": 1024}

        def row(n):
            return rowt.ap()[:, ROW[n] : ROW[n] + 256]

        def scr(i):
            return scrA.ap()[:, (i % 8) * BLK : (i % 8 + 1) * BLK]

        def scrb(i):
            return bufs[f"scr{i % 8}"]

        const_pairs = [(identt, identc), (rampt, rampc), (idx0t, idx0c), (idx1t, idx1c),
                       (kcolt, kcolc), (h128t, h128c),
                       (h64dt, h64dc), (scolt, scols), (normwt, normw)]
        for dst, srcc in const_pairs:
            p.op("q", lambda h, d=dst, s=srcc: h.dma_start(out=d.ap()[:], in_=s[:]),
                 writes=[bufs["consts"]], sem="qc")
        p.op("q", lambda h: h.dma_start(out=scrA.ap()[:, 0:NCH], in_=cT[:]),
             writes=[bufs["scr0"]], sem="qc")
        qcfin = ("qc", p.cnt["qc"])
        bufs["consts"].mark_ready(qcfin)
        bufs["scr0"].mark_ready(qcfin)
        p.op("v", lambda h: h.memset(onest.ap()[:], 1.0), writes=[bufs["consts"]])
        p.op("v", lambda h: h.memset(eps5c.ap()[:], 1e-5), writes=[bufs["consts"]])
        p.op("v", lambda h: h.memset(eps6x.ap()[:], 1e-6 * WS * WS), writes=[bufs["consts"]])
        p.op("v", lambda h: h.memset(kv_t[:, :], 0.0), writes=[bufs[f"kv{hh}"] for hh in range(NH)])
        p.op("v", lambda h: h.memset(rowt.ap()[:, 0:96], 0.0), writes=[bufs["r_sum"]])
        p.op("s", lambda h: h.mul(out=negst.ap()[:], in_=scolt.ap()[:], mul=-1.0),
             reads=[bufs["consts"]], writes=[bufs["decays"]])
        p.op("s", lambda h: h.activation(out=cst.ap()[:], in_=scrA.ap()[:, 0:NCH], func=AF.Silu),
             reads=[bufs["scr0"]], writes=[bufs["cst"]])
        for h_ in range(NH):
            p.op("s", lambda h, i=h_: h.activation(out=kdect.ap()[:, 2 * i : 2 * i + 2],
                                                   in_=kcolt.ap()[:], func=AF.Exp,
                                                   scale=negst.ap()[:, i : i + 1]),
                 reads=[bufs["consts"], bufs["decays"]], writes=[bufs["decays"]])
            p.op("s", lambda h, i=h_: h.activation(out=bdct.ap()[:, i : i + 1],
                                                   in_=scolt.ap()[:, i : i + 1], func=AF.Exp,
                                                   scale=-float(BLK)),
                 reads=[bufs["consts"]], writes=[bufs["consts" if False else "decays"]])

        def emit_split_slab(src, j, dst, stsem=None):
                si = j % 2
                if stsem is None:
                    stsem = f"gs{si}"
                sl = pC if si == 0 else pD
                slb = bufs["pC" if si == 0 else "pD"]
                ob = wsl16[si]
                obb = bufs[f"wslh{si}"]
                obb2 = bufs[f"wsll{si}"]
                m1 = p.op("q", lambda h, d=sl, s=src, c=j * 128: h.dma_start(
                    out=d[:, 0:2048].rearrange("p (c w) -> p c w", c=NCH),
                    in_=s.rearrange("(c p) x -> p c x", p=128)[:, :, c : c + 128]),
                    writes=[slb], sem=f"qu{si}")
                p.op("s", lambda h, d=sl, o=ob: h.activation(
                    out=o.ap()[:, 0:2048], in_=d[:, 0:2048], func=AF.Identity, scale=WS),
                    reads=[slb], writes=[obb, obb2], extra=[m1])
                mstt = p.op("v", lambda h, d=sl, o=ob: h.scalar_tensor_tensor(
                    out=o.ap()[:, 2048:4096], in0=d[:, 0:2048], scalar=WS, in1=o.ap()[:, 0:2048],
                    op0=ALU.mult, op1=ALU.subtract),
                    reads=[slb, obb], writes=[obb, obb2])
                p.op("g", lambda h, o=ob, dd=dst, j_=j: h.dma_start(out=dd[j_, :, :], in_=o.ap()[:, 0:4096]),
                     reads=[obb, obb2], writes=[], sem=stsem)
                return si, mstt

        def emit_split(src, nslab, dst, wsbuf, semfn=None, qbufs=None, qsize=0):
            qsems = set()
            for j in range(nslab):
                sm = semfn(j) if semfn else None
                emit_split_slab(src, j, dst, stsem=sm)
                if sm is not None:
                    qsems.add(sm)
                if qbufs is not None and (j + 1) % qsize == 0:
                    qb = Buf()
                    qb.ready = [(s_, p.cnt[s_]) for s_ in sorted(qsems)]
                    qbufs.append(qb)
            if semfn is None:
                wsbuf.ready = [("gs0", p.cnt["gs0"]), ("gs1", p.cnt["gs1"])]
            else:
                sms = sorted({semfn(j) for j in range(nslab)})
                wsbuf.ready = [(sm, p.cnt[sm]) for sm in sms]

        def foldA_iter(uc):
            si = uc % 2
            sl = pA if si == 0 else pB
            slb = bufs["pA" if si == 0 else "pB"]
            ld = p.op("q", lambda h, d=sl, r0=uc * 128:
                      h.dma_start(out=d[:, 0:2048], in_=w2t[r0 : r0 + 128, :]),
                      writes=[slb], sem=f"qw{si}")
            ebuf = scrA.ap()[:, 0:2048] if uc % 2 == 0 else hz.ap()[:, 0:2048]
            ebufb = [bufs[f"scr{i}"] for i in range(8)] if uc % 2 == 0 else [bufs["hz"]]
            for dt_ in range(4):
                pbk = pb[dt_ % 2]
                pbkb = bufs[f"pb{dt_ % 2}"]
                p.mm_group([lambda h, d=sl, t0=dt_ * 512, pk=pbk:
                            h.matmul(pk.ap()[:, 0:512], h128t.ap()[:], d[:, t0 : t0 + 512],
                                     start=True, stop=True)],
                           reads=[slb, bufs["consts"]], writes=[pbkb], extra=[ld])
                if dt_ % 2 == 0:
                    p.op("v", lambda h, e=ebuf, t0=dt_ * 512, pk=pbk:
                         h.tensor_copy(out=e[:, t0 : t0 + 512], in_=pk.ap()[:, 0:512]),
                         reads=[pbkb], writes=ebufb)
                else:
                    p.op("s", lambda h, e=ebuf, t0=dt_ * 512, pk=pbk:
                         h.copy(out=e[:, t0 : t0 + 512], in_=pk.ap()[:, 0:512]),
                         reads=[pbkb], writes=ebufb)
            p.op("g", lambda h, e=ebuf, r0=uc * 128:
                 h.dma_start(out=tAd[r0 : r0 + 128, :], in_=e[:, 0:2048]),
                 reads=ebufb, writes=[bufs["tAd"]])

        def foldB_iter(vg):
            si = vg % 2
            sl = pA if si == 0 else pB
            slb = bufs["pA" if si == 0 else "pB"]
            l1 = p.op("q", lambda h, d=sl, v0=2 * vg:
                      h.dma_start(out=d[0:64, 0:2048],
                                  in_=tAd.rearrange("(u v) d -> v u d", v=128)[v0, :, :]),
                      reads=[bufs["tAd"]], writes=[slb], sem=f"qw{si}")
            l2 = p.op("q", lambda h, d=sl, v0=2 * vg + 1:
                      h.dma_start(out=d[64:128, 0:2048],
                                  in_=tAd.rearrange("(u v) d -> v u d", v=128)[v0, :, :]),
                      reads=[bufs["tAd"]], writes=[slb], sem=f"qw{si}")
            slb.ready = [l2]
            ebuf = scrA.ap()[:, 0:2048] if vg % 2 == 0 else hz.ap()[:, 0:2048]
            ebufb = [bufs[f"scr{i}"] for i in range(8)] if vg % 2 == 0 else [bufs["hz"]]
            fsp = hsp.ap()[:, 0:4096] if vg % 2 == 0 else hsp.ap()[:, 4096:8192]
            fspb = bufs["hs"]
            for dt_ in range(4):
                pbk = pb[2 + dt_ % 2]
                pbkb = bufs[f"pb{2 + dt_ % 2}"]
                p.mm_group([lambda h, d=sl, t0=dt_ * 512, pk=pbk:
                            h.matmul(pk.ap()[:, 0:512], h64dt.ap()[:], d[:, t0 : t0 + 512],
                                     start=True, stop=True)],
                           reads=[slb, bufs["consts"]], writes=[pbkb], extra=[l2])
                if dt_ % 2 == 0:
                    p.op("v", lambda h, e=ebuf, t0=dt_ * 512, pk=pbk:
                         h.tensor_copy(out=e[:, t0 : t0 + 512], in_=pk.ap()[:, 0:512]),
                         reads=[pbkb], writes=ebufb)
                else:
                    p.op("s", lambda h, e=ebuf, t0=dt_ * 512, pk=pbk:
                         h.copy(out=e[:, t0 : t0 + 512], in_=pk.ap()[:, 0:512]),
                         reads=[pbkb], writes=ebufb)
                p.op("s", lambda h, e=ebuf, f=fsp, t0=dt_ * 512: h.activation(
                    out=f[:, t0 : t0 + 512], in_=e[:, t0 : t0 + 512], func=AF.Identity, scale=WS),
                    reads=ebufb, writes=[fspb])
                p.op("v", lambda h, e=ebuf, f=fsp, t0=dt_ * 512: h.scalar_tensor_tensor(
                    out=f[:, 2048 + t0 : 2048 + t0 + 512], in0=e[:, t0 : t0 + 512], scalar=WS,
                    in1=f[:, t0 : t0 + 512], op0=ALU.mult, op1=ALU.subtract),
                    reads=ebufb + [fspb], writes=[fspb])
            p.op("g", lambda h, f=fsp, v0=2 * vg:
                 h.dma_start(out=w2f_s.rearrange("(u v) d -> v u d", v=128)[v0, :, :],
                             in_=f[0:64, :]),
                 reads=[fspb], writes=[], sem="gs2")
            p.op("g", lambda h, f=fsp, v0=2 * vg + 1:
                 h.dma_start(out=w2f_s.rearrange("(u v) d -> v u d", v=128)[v0, :, :],
                             in_=f[64:128, :]),
                 reads=[fspb], writes=[], sem="gs3")

        def emit_ln_stats(x_ap, xbuf, par7):
            inv_d = 1.0 / float(D)
            fns = [lambda h, d=dc_: h.matmul(pb[6].ap()[0:1, 0:256], onest.ap()[:, 0:1],
                                             x_ap[:, d * BLK : (d + 1) * BLK],
                                             start=(d == 0), stop=(d == NCH - 1))
                   for dc_ in range(NCH)]
            p.mm_group(fns, reads=[xbuf, bufs["consts"]], writes=[bufs["pb6"]])
            p.op("v", lambda h: h.tensor_copy(out=row("r_sum"), in_=pb[6].ap()[0:1, 0:256]),
                 reads=[bufs["pb6"]], writes=[bufs["r_sum"]])
            sqb = 7 if par7 else 6
            sqbb = bufs[f"pb{sqb}"]
            for half in range(2):
                for dcl in range(8):
                    dc = half * 8 + dcl
                    p.op("s", lambda h, d=dc: h.activation(out=scr(d), in_=x_ap[:, d * BLK : (d + 1) * BLK],
                                                           func=AF.Square),
                         reads=[xbuf], writes=[scrb(dc)])
                fns = [lambda h, d=half * 8 + dcl_: h.matmul(pb[sqb].ap()[0:1, 0:256], onest.ap()[:, 0:1],
                                                             scrA.ap()[:, (d % 8) * BLK : (d % 8 + 1) * BLK],
                                                             start=(d == 0), stop=(d == NCH - 1))
                       for dcl_ in range(8)]
                p.mm_group(fns, reads=[scrb(i) for i in range(8)] + [bufs["consts"]],
                           writes=([sqbb] if half == 0 else []))
            sqbb.mark_ready(("t", p.cnt["t"]))
            p.op("v", lambda h, b_sq=sqb: h.tensor_copy(out=row("r_sq"), in_=pb[b_sq].ap()[0:1, 0:256]),
                 reads=[sqbb], writes=[bufs["r_sq"]])
            p.op("s", lambda h: h.mul(out=row("r_a"), in_=row("r_sum"), mul=inv_d),
                 reads=[bufs["r_sum"]], writes=[bufs["r_a"]])
            p.op("v", lambda h: h.tensor_mul(out=row("r_b"), in0=row("r_a"), in1=row("r_a")),
                 reads=[bufs["r_a"]], writes=[bufs["r_b"]])
            p.op("v", lambda h: h.scalar_tensor_tensor(out=row("r_sum"), in0=row("r_sq"),
                                                       scalar=inv_d, in1=row("r_b"),
                                                       op0=ALU.mult, op1=ALU.subtract),
                 reads=[bufs["r_sq"], bufs["r_b"]], writes=[bufs["r_sum"]])
            p.op("s", lambda h: h.activation(out=row("r_b"), in_=row("r_sum"), func=AF.Sqrt,
                                             bias=eps5c.ap()[0:1, :]),
                 reads=[bufs["r_sum"], bufs["consts"]], writes=[bufs["r_b"]])
            p.op("v", lambda h: h.reciprocal(out=row("r_rstd"), in_=row("r_b")),
                 reads=[bufs["r_b"]], writes=[bufs["r_rstd"]])
            p.op("v", lambda h: h.tensor_mul(out=row("r_mr"), in0=row("r_a"), in1=row("r_rstd")),
                 reads=[bufs["r_a"], bufs["r_rstd"]], writes=[bufs["r_mr"]])
            p.mm_group([lambda h: h.matmul(pb[6].ap()[:, 0:256], onest.ap()[0:1, 0:128],
                                           row("r_rstd"), start=True, stop=True)],
                       reads=[bufs["r_rstd"], bufs["consts"]], writes=[bufs["pb6"]])
            p.op("v", lambda h: h.tensor_copy(out=rstdR.ap()[:], in_=pb[6].ap()[:, 0:256]),
                 reads=[bufs["pb6"]], writes=[bufs["rstdR"]])
            p.mm_group([lambda h, b_sq=sqb: h.matmul(pb[b_sq].ap()[:, 0:256], onest.ap()[0:1, 0:128],
                                                     row("r_mr"), start=True, stop=True)],
                       reads=[bufs["r_mr"], bufs["consts"]], writes=[sqbb])
            p.op("v", lambda h, b_sq=sqb: h.tensor_copy(out=mrR.ap()[:], in_=pb[b_sq].ap()[:, 0:256]),
                 reads=[sqbb], writes=[bufs["mrR"]])

        def emit_ln_apply(x_ap, xbuf, capcol, shift_off, mbufs=None):
            if mbufs is None:
                mbufs = [bufs["modpost"], bufs["modct"]]
            marks = []
            for dc in range(NCH):
                sl_ = slice(dc * BLK, (dc + 1) * BLK)
                p.op("v", lambda h, s=sl_, d=dc: h.tensor_mul(out=scr(d), in0=x_ap[:, s],
                                                              in1=rstdR.ap()[:]),
                     reads=[xbuf, bufs["rstdR"]], writes=[scrb(dc)])
                p.op("v", lambda h, d=dc: h.tensor_sub(out=scr(d), in0=scr(d), in1=mrR.ap()[:]),
                     reads=[scrb(dc), bufs["mrR"]], writes=[scrb(dc)])
                p.op("s", lambda h, s=sl_, d=dc, so=shift_off: h.activation(
                    out=hz.ap()[:, s], in_=scr(d), func=AF.Identity,
                    scale=capcol.ap()[:, d : d + 1], bias=modct.ap()[:, so + d : so + d + 1]),
                    reads=[scrb(dc)] + mbufs, writes=[bufs["hz"]])
                p.op("s", lambda h, s=sl_: h.copy(out=hh_t[:, s], in_=hz.ap()[:, s]),
                     reads=[bufs["hz"]], writes=[bufs["hs"]])
                m_ = p.op("v", lambda h, s=sl_: h.tensor_sub(out=hl_t[:, s], in0=hz.ap()[:, s],
                                                             in1=hh_t[:, s]),
                          reads=[bufs["hz"], bufs["hs"]], writes=[bufs["hs"]])
                marks.append(m_)
            return marks

        def load_x(par, b_idx, srcd, x2buf=None):
            dst = xb if par == 0 else xbB
            xbuf = bufs[f"xb{par}"]
            t = b_idx * BLK
            reads = [x2buf] if x2buf is not None else []
            sm = "qx" if par == 0 else "qx1"
            p.op("q", lambda h, t_=t, d=dst: h.dma_start(
                out=d.ap().rearrange("p (c w) -> p c w", c=NCH)[:, 0:8, :],
                in_=srcd.rearrange("(c p) x -> p c x", p=128)[:, 0:8, t_ : t_ + BLK]),
                reads=reads, writes=[xbuf], sem=sm)
            p.op("q", lambda h, t_=t, d=dst: h.dma_start(
                out=d.ap().rearrange("p (c w) -> p c w", c=NCH)[:, 8:16, :],
                in_=srcd.rearrange("(c p) x -> p c x", p=128)[:, 8:16, t_ : t_ + BLK]),
                reads=reads, writes=[xbuf], sem=sm)

        SLOTB = ("wslh0", "wslh1", "wsll0", "wsll1")

        def load_oc(slot, srcs, j, wsbuf):
            off = (slot % 2) * 4096
            ob = wsl16[slot // 2].ap()[:, off : off + 4096]
            obb = bufs[SLOTB[slot]]
            m1 = p.op("q", lambda h, d=ob, s=srcs, j_=j: h.dma_start(out=d, in_=s[j_, :, :]),
                      reads=[wsbuf], writes=[obb], sem=f"qw{slot}")
            return ob, obb, [m1]

        def mm_group_tracked(pk, pkb, wv, wb, lds, hmarks):
            p.deps("t", reads=[wb], writes=[pkb], extra=lds)
            for dc in range(NCH):
                p._wait("t", hmarks[dc])
                w0 = dc * 128
                hs_ = slice(dc * 256, dc * 256 + 256)
                first = dc == 0
                last = dc == NCH - 1
                p.raw("t", lambda h, d=wv, w=w0, s=hs_, f=first: h.matmul(
                    pk.ap()[:, 0:256], d[:, w : w + 128], hh_t[:, s], start=f, stop=False))
                p.raw("t", lambda h, d=wv, w=w0, s=hs_: h.matmul(
                    pk.ap()[:, 0:256], d[:, w : w + 128], hl_t[:, s], start=False, stop=False))
                fn = (lambda h, d=wv, w=w0, s=hs_, l=last: h.matmul(
                    pk.ap()[:, 0:256], d[:, 2048 + w : 2048 + w + 128], hh_t[:, s],
                    start=False, stop=l))
                if last:
                    mark = p.inc("t", fn)
                else:
                    p.raw("t", fn)
            wb.mark_user(mark)
            bufs["hs"].mark_user(mark)
            pkb.mark_ready(mark)
            return mark

        def fns3_optA(pk, wv):
            fns = []
            for dc in range(NCH):
                w0 = dc * 128
                hs_ = slice(dc * 256, dc * 256 + 256)
                first = dc == 0
                last = dc == NCH - 1
                fns.append(lambda h, d=wv, w=w0, s=hs_, f=first: h.matmul(
                    pk.ap()[:, 0:256], d[:, w : w + 128], hh_t[:, s], start=f, stop=False))
                fns.append(lambda h, d=wv, w=w0, s=hs_: h.matmul(
                    pk.ap()[:, 0:256], d[:, w : w + 128], hl_t[:, s], start=False, stop=False))
                fns.append(lambda h, d=wv, w=w0, s=hs_, l=last: h.matmul(
                    pk.ap()[:, 0:256], d[:, 2048 + w : 2048 + w + 128], hh_t[:, s],
                    start=False, stop=l))
            return fns

        p.mm_group([lambda h: h.matmul(pb[0].ap()[:, 0:32], onest.ap()[0:1, 0:128],
                                       rowt.ap()[0:1, 0:32], start=True, stop=True)],
                   reads=[bufs["consts"], bufs["r_sum"]], writes=[bufs["pb0"]])
        p.mm_group([lambda h: h.matmul(pb[1].ap()[:, 0:64], onest.ap()[0:1, 0:128],
                                       rowt.ap()[0:1, 0:64], start=True, stop=True)],
                   reads=[bufs["consts"], bufs["r_sum"]], writes=[bufs["pb1"]])
        if stages in ("all", "attn"):
            load_x(0, 0, xT)
            emit_ln_stats(xb.ap(), bufs["xb0"], par7=False)

        def emit_mod_window(w):
            tgt = pb[0] if w < 2 else pb[1]
            base = 0 if w < 2 else 32
            for dc in range(NCH):
                si = dc % 2
                sl = pA if si == 0 else pB
                slb = bufs["pA" if si == 0 else "pB"]
                ld = p.op("q", lambda h, d=sl, r0=dc * 128, c0=w * 2048:
                          h.dma_start(out=d[:, 0:2048], in_=wmod[r0 : r0 + 128, c0 : c0 + 2048]),
                          writes=[slb], sem=f"qm{si}")
                fns = []
                for jl in range(16):
                    ja = w * 16 + jl - base
                    fns.append(lambda h, d=sl, jl_=jl, ja_=ja, dc_=dc, t_=tgt:
                               h.matmul(t_.ap()[:, ja_ : ja_ + 1], d[:, jl_ * 128 : jl_ * 128 + 128],
                                        cst.ap()[:, dc_ : dc_ + 1], start=False, stop=False,
                                        skip_group_check=True))
                p.mm_group(fns, reads=[slb, bufs["cst"]], writes=[], extra=[ld])

        emit_mod_window(0)
        emit_mod_window(1)
        bufs["pb0"].mark_ready(("t", p.cnt["t"]))
        p.op("v", lambda h: h.tensor_copy(out=modct.ap()[:, 0:32], in_=pb[0].ap()[:, 0:32]),
             reads=[bufs["pb0"]], writes=[bufs["modct"]])
        p.op("s", lambda h: h.add(out=capc.ap()[:], in_=modct.ap()[:, 16:32], add=1.0),
             reads=[bufs["modct"]], writes=[bufs["modpost"]])

        wq_qbufs = []

        def emit_quarter(qi):
            for j in range(qi * 12, qi * 12 + 12):
                emit_split_slab(wq, j, wq_s, stsem=f"gq{qi * 2 + j % 2}")
            qb = Buf()
            qb.ready = [(f"gq{qi * 2}", p.cnt[f"gq{qi * 2}"]),
                        (f"gq{qi * 2 + 1}", p.cnt[f"gq{qi * 2 + 1}"])]
            wq_qbufs.append(qb)

        emit_quarter(0)
        hmarks0 = (emit_ln_apply(xb.ap(), bufs["xb0"], capc, 0)
                   if stages in ("all", "attn") else None)
        for w_, qi_ in ((2, 1), (3, 2), (4, 3)):
            emit_mod_window(w_)
            emit_quarter(qi_)
        emit_mod_window(5)
        bufs["pb1"].mark_ready(("t", p.cnt["t"]))
        bufs["wqS"].ready = [(f"gq{i}", p.cnt[f"gq{i}"]) for i in range(8)]
        for j in range(16):
            emit_split_slab(wg, j, wg_s, stsem=f"gw{j % 2}")
        bufs["wgS"].ready = [("gw0", p.cnt["gw0"]), ("gw1", p.cnt["gw1"])]
        for j in range(16):
            emit_split_slab(wo, j, wo_s, stsem=f"gw{2 + j % 2}")
        bufs["woS"].ready = [("gw2", p.cnt["gw2"]), ("gw3", p.cnt["gw3"])]

        p.op("v", lambda h: h.tensor_copy(out=modct.ap()[:, 32:96], in_=pb[1].ap()[:, 0:64]),
             reads=[bufs["pb1"]], writes=[bufs["modctB"]])
        p.op("s", lambda h: h.add(out=cbpc.ap()[:], in_=modct.ap()[:, 64:80], add=1.0),
             reads=[bufs["modctB"]], writes=[bufs["modpostB"]])
        for (g_src, g_dst, g_tmp) in ((32, gac, tmga), (80, gbc, tmgb)):
            p.op("v", lambda h, s0=g_src, t=g_tmp: h.tensor_scalar(
                out=t.ap()[:], in0=modct.ap()[:, s0 : s0 + 16], scalar1=0.0, scalar2=None,
                op0=ALU.is_ge), reads=[bufs["modctB"]], writes=[bufs["modpostB"]])
            p.op("v", lambda h, t=g_tmp: h.tensor_scalar(
                out=t.ap()[:], in0=t.ap()[:], scalar1=2.0, scalar2=-1.0,
                op0=ALU.mult, op1=ALU.add), reads=[bufs["modpostB"]], writes=[bufs["modpostB"]])
            p.op("v", lambda h, s0=g_src, d=g_dst: h.tensor_scalar(
                out=d.ap()[:], in0=modct.ap()[:, s0 : s0 + 16], scalar1=-1.0, scalar2=None,
                op0=ALU.mult), reads=[bufs["modctB"]], writes=[bufs["modpostB"]])
            p.op("v", lambda h, s0=g_src, d=g_dst: h.tensor_max(
                out=d.ap()[:], in0=d.ap()[:], in1=modct.ap()[:, s0 : s0 + 16]),
                reads=[bufs["modctB"], bufs["modpostB"]], writes=[bufs["modpostB"]])
            p.op("v", lambda h, d=g_dst: h.tensor_scalar(
                out=d.ap()[:], in0=d.ap()[:], scalar1=0.25, scalar2=None, op0=ALU.max),
                reads=[bufs["modpostB"]], writes=[bufs["modpostB"]])
            p.op("v", lambda h, d=g_dst, t=g_tmp: h.tensor_mul(
                out=d.ap()[:], in0=d.ap()[:], in1=t.ap()[:]),
                reads=[bufs["modpostB"]], writes=[bufs["modpostB"]])
        p.op("v", lambda h: h.tensor_scalar(out=gbc.ap()[:], in0=gbc.ap()[:],
                                            scalar1=1.0 / (WS * WS), scalar2=None, op0=ALU.mult),
             reads=[bufs["modpostB"]], writes=[bufs["modpostB"]])

        for b_ in (range(NBLK) if stages in ("all", "attn") else range(0)):
            t0 = b_ * BLK
            par = b_ % 2
            xcur = xb if par == 0 else xbB
            xcur_ap = xcur.ap()
            xcurb = bufs[f"xb{par}"]
            hmarks = hmarks0 if b_ == 0 else emit_ln_apply(xcur_ap, xcurb, capc, 0)

            for oc48 in range(48):
                wsb = wq_qbufs[oc48 // 12] if b_ == 0 else bufs["wqS"]
                wv, wb, lds = load_oc(oc48 % 4, wq_s, oc48, wsb)
                if oc48 < 32:
                    dest_ap = q_t if oc48 < 16 else k_t
                    dest_b = bufs["q"] if oc48 < 16 else bufs["k"]
                    oc = oc48 % 16
                    pk = pb[oc % 2]
                    pkb = bufs[f"pb{oc % 2}"]
                    if oc48 == 0:
                        mm_group_tracked(pk, pkb, wv, wb, lds, hmarks)
                    else:
                        p.mm_group(fns3_optA(pk, wv), reads=[wb, bufs["hs"]], writes=[pkb], extra=lds)
                    p.op("s", lambda h, da=dest_ap, o=oc, pk_=pk: h.activation(
                        out=da[:, o * 256 : (o + 1) * 256], in_=pk_.ap()[:, 0:256],
                        func=AF.Silu, scale=IWS),
                        reads=[pkb], writes=[dest_b])
                else:
                    hd = oc48 - 32
                    for th in range(2):
                        pk = pb[th]
                        pkb = bufs[f"pb{th}"]
                        fns = []
                        for dc in range(NCH):
                            h0 = dc * 256 + th * 128
                            first = dc == 0
                            last = dc == NCH - 1
                            fns.append(lambda h, hb_=h0, w=dc * 128, d=wv, f=first, pk_=pk: h.matmul(
                                pk_.ap()[:, 0:128], hh_t[:, hb_ : hb_ + 128], d[:, w : w + 128],
                                start=f, stop=False))
                            fns.append(lambda h, hb_=h0, w=dc * 128, d=wv, pk_=pk: h.matmul(
                                pk_.ap()[:, 0:128], hl_t[:, hb_ : hb_ + 128], d[:, w : w + 128],
                                start=False, stop=False))
                            fns.append(lambda h, hb_=h0, w=2048 + dc * 128, d=wv, l=last, pk_=pk: h.matmul(
                                pk_.ap()[:, 0:128], hh_t[:, hb_ : hb_ + 128], d[:, w : w + 128],
                                start=False, stop=l))
                        p.mm_group(fns, reads=[wb, bufs["hs"]], writes=[pkb], extra=lds)
                        vdst = vt0 if th == 0 else vt1
                        vdb = bufs["vt0" if th == 0 else "vt1"]
                        p.op("s", lambda h, vd=vdst, hd_=hd, pk_=pk: h.activation(
                            out=vd[:, hd_ * 128 : hd_ * 128 + 128],
                            in_=pk_.ap()[:, 0:128], func=AF.Silu, scale=IWS),
                            reads=[pkb], writes=[vdb])

            for oc in range(16):
                wv, wb, lds = load_oc(oc % 4, wg_s, oc, bufs["wgS"])
                pk = pb[oc % 2]
                pkb = bufs[f"pb{oc % 2}"]
                p.mm_group(fns3_optA(pk, wv), reads=[wb, bufs["hs"]], writes=[pkb], extra=lds)
                p.op("s", lambda h, o=oc, pk_=pk: h.activation(
                    out=gate_t[:, o * 256 : (o + 1) * 256], in_=pk_.ap()[:, 0:256],
                    func=AF.Sigmoid, scale=IWS), reads=[pkb], writes=[bufs["gate"]])

            if b_ + 1 < NBLK:
                load_x((b_ + 1) % 2, b_ + 1, xT)
                nxt = xb if (b_ + 1) % 2 == 0 else xbB
                emit_ln_stats(nxt.ap(), bufs[f"xb{(b_ + 1) % 2}"], par7=False)

            for h_ in range(NH):
                so = slot_off[h_ % 2]
                slotb = bufs[f"slot{h_ % 2}"]
                sqdec = ar3.ap()[:, so + 0 : so + 256]
                sdd0 = ar3.ap()[:, so + 256 : so + 512]
                sdd1 = ar3.ap()[:, so + 512 : so + 768]
                sqkd0 = sdd0
                sqkd1 = sdd1
                sqs = ar3.ap()[:, so + 768 : so + 1024]
                sk2 = ar3.ap()[:, so + 1024 : so + 1280]
                kvh = kv_t[:, h_ * HD : (h_ + 1) * HD]
                kvb = bufs[f"kv{h_}"]
                qch = q_t[:, h_ * 256 : (h_ + 1) * 256]
                kch = k_t[:, h_ * 256 : (h_ + 1) * 256]

                p.op("s", lambda h, d=sqdec, i=h_: h.activation(out=d, in_=rampt.ap()[:],
                                                                func=AF.Exp,
                                                                scale=negst.ap()[:, i : i + 1]),
                     reads=[bufs["consts"], bufs["decays"]], writes=[slotb])
                p.op("s", lambda h, d=sdd0, i=h_: h.activation(out=d, in_=idx0t.ap()[:],
                                                               func=AF.Exp,
                                                               scale=negst.ap()[:, i : i + 1]),
                     reads=[bufs["consts"], bufs["decays"]], writes=[slotb])
                p.op("s", lambda h, d=sdd1, i=h_: h.activation(out=d, in_=idx1t.ap()[:],
                                                               func=AF.Exp,
                                                               scale=negst.ap()[:, i : i + 1]),
                     reads=[bufs["consts"], bufs["decays"]], writes=[slotb])
                p.mm_group([lambda h, kc=kch, qc=qch: h.matmul(pb[2].ap()[:, 0:256],
                                                               kc[:, 0:128], qc,
                                                               start=True, stop=True)],
                           reads=[bufs["k"], bufs["q"]], writes=[bufs["pb2"]])
                p.mm_group([lambda h, kc=kch, qc=qch: h.matmul(pb[3].ap()[:, 0:256],
                                                               kc[:, 128:256], qc,
                                                               start=True, stop=True)],
                           reads=[bufs["k"], bufs["q"]], writes=[bufs["pb3"]])
                p.op("v", lambda h, d=sqkd0, dd=sdd0: h.tensor_mul(out=d, in0=pb[2].ap()[:, 0:256], in1=dd),
                     reads=[bufs["pb2"], slotb], writes=[slotb])
                p.op("v", lambda h, d=sqkd1, dd=sdd1: h.tensor_mul(out=d, in0=pb[3].ap()[:, 0:256], in1=dd),
                     reads=[bufs["pb3"], slotb], writes=[slotb])
                p.op("v", lambda h, d=sqs, qc=qch, dec=sqdec: h.tensor_mul(out=d, in0=qc, in1=dec),
                     reads=[bufs["q"], slotb], writes=[slotb])
                p.mm_group([
                    lambda h, kv_=kvh, d=sqs: h.matmul(pb[4].ap()[:, 0:256], kv_, d,
                                                       start=True, stop=False),
                    lambda h, i=h_, d=sqkd0: h.matmul(pb[4].ap()[:, 0:256],
                                                      vt0[:, i * HD : (i + 1) * HD], d,
                                                      start=False, stop=False),
                    lambda h, i=h_, d=sqkd1: h.matmul(pb[4].ap()[:, 0:256],
                                                      vt1[:, i * HD : (i + 1) * HD], d,
                                                      start=False, stop=True),
                ], reads=[kvb, slotb, bufs["vt0"], bufs["vt1"]], writes=[bufs["pb4"]])
                p.mm_group([lambda h, kc=kch: h.transpose(pb[5].ap()[:, 0:128], kc[:, 0:128],
                                                          identt.ap()[:])],
                           reads=[bufs["k"], bufs["consts"]], writes=[bufs["pb5"]])
                p.op("s", lambda h, d=sk2, i=h_: h.activation(out=d[:, 0:128], in_=pb[5].ap()[:, 0:128],
                                                              func=AF.Identity,
                                                              scale=kdect.ap()[:, 2 * i : 2 * i + 1]),
                     reads=[bufs["pb5"], bufs["decays"]], writes=[slotb])
                p.mm_group([lambda h, kc=kch: h.transpose(pb[5].ap()[:, 128:256], kc[:, 128:256],
                                                          identt.ap()[:])],
                           reads=[bufs["k"], bufs["consts"]], writes=[bufs["pb5"]])
                p.op("s", lambda h, d=sk2, i=h_: h.activation(out=d[:, 128:256], in_=pb[5].ap()[:, 128:256],
                                                              func=AF.Identity,
                                                              scale=kdect.ap()[:, 2 * i + 1 : 2 * i + 2]),
                     reads=[bufs["pb5"], bufs["decays"]], writes=[slotb])
                p.mm_group([
                    lambda h, d=sk2, i=h_: h.matmul(pb[5].ap()[:, 256:384], d[:, 0:128],
                                                    vt0[:, i * HD : (i + 1) * HD],
                                                    start=True, stop=False),
                    lambda h, d=sk2, i=h_: h.matmul(pb[5].ap()[:, 256:384], d[:, 128:256],
                                                    vt1[:, i * HD : (i + 1) * HD],
                                                    start=False, stop=True),
                ], reads=[slotb, bufs["vt0"], bufs["vt1"]], writes=[bufs["pb5"]])
                p.op("v", lambda h, kv_=kvh, i=h_: h.scalar_tensor_tensor(
                    out=kv_, in0=kv_, scalar=bdct.ap()[:, i : i + 1], in1=pb[5].ap()[:, 256:384],
                    op0=ALU.mult, op1=ALU.add),
                    reads=[kvb, bufs["pb5"], bufs["decays"]], writes=[kvb])
                m_osq = p.op("s", lambda h, d=sqs: h.activation(out=d, in_=pb[4].ap()[:, 0:256],
                                                                func=AF.Square),
                             reads=[bufs["pb4"]], writes=[slotb])
                p.mm_group([lambda h, d=sqs, i=h_: h.matmul(pb[7].ap()[0:1, 0:256], onest.ap()[:, 0:1],
                                                            d, start=(i == 0), stop=(i == NH - 1))],
                           reads=[slotb, bufs["consts"]],
                           writes=([bufs["pb7"]] if h_ in (0, NH - 1) else []))
                p.op("v", lambda h, i=h_: h.scalar_tensor_tensor(
                    out=hz.ap()[:, i * 256 : (i + 1) * 256],
                    in0=gate_t[:, i * 256 : (i + 1) * 256],
                    scalar=normwt.ap()[:, i : i + 1], in1=pb[4].ap()[:, 0:256],
                    op0=ALU.mult, op1=ALU.mult),
                    reads=[bufs["pb4"], bufs["gate"], bufs["consts"]], writes=[bufs["hz"]],
                    extra=[m_osq])
                p.op("v", lambda h, i=h_: h.tensor_copy(out=hh_t[:, i * 256 : (i + 1) * 256],
                                                        in_=hz.ap()[:, i * 256 : (i + 1) * 256]),
                     reads=[bufs["hz"]], writes=[bufs["hs"]])
                p.op("v", lambda h, i=h_: h.tensor_sub(out=hl_t[:, i * 256 : (i + 1) * 256],
                                                       in0=hz.ap()[:, i * 256 : (i + 1) * 256],
                                                       in1=hh_t[:, i * 256 : (i + 1) * 256]),
                     reads=[bufs["hz"], bufs["hs"]], writes=[bufs["hs"]])

            p.op("v", lambda h: h.tensor_copy(out=row("r_rsq"), in_=pb[7].ap()[0:1, 0:256]),
                 reads=[bufs["pb7"]], writes=[bufs["r_rsq"]])
            p.op("s", lambda h: h.activation(out=row("r_rtmp"), in_=row("r_rsq"), func=AF.Sqrt,
                                             scale=WS * WS / float(D), bias=eps6x.ap()[0:1, :]),
                 reads=[bufs["r_rsq"], bufs["consts"]], writes=[bufs["r_rtmp"]])
            p.op("v", lambda h: h.reciprocal(out=row("r_rrstd"), in_=row("r_rtmp")),
                 reads=[bufs["r_rtmp"]], writes=[bufs["r_rrstd"]])
            p.mm_group([lambda h: h.matmul(pb[7].ap()[:, 256:512], onest.ap()[0:1, 0:128],
                                           row("r_rrstd"), start=True, stop=True)],
                       reads=[bufs["r_rrstd"], bufs["consts"]], writes=[bufs["pb7"]])
            p.op("v", lambda h: h.tensor_copy(out=rmsR.ap()[:], in_=pb[7].ap()[:, 256:512]),
                 reads=[bufs["pb7"]], writes=[bufs["rmsR"]])

            for fc in range(16):
                wv, wb, lds = load_oc(fc % 4, wo_s, fc, bufs["woS"])
                pk = pb[fc % 2]
                pkb = bufs[f"pb{fc % 2}"]
                p.mm_group(fns3_optA(pk, wv), reads=[wb, bufs["hs"]], writes=[pkb], extra=lds)
                p.op("v", lambda h, f=fc, pk_=pk: h.tensor_mul(
                    out=scr(f), in0=pk_.ap()[:, 0:256], in1=rmsR.ap()[:]),
                    reads=[pkb, bufs["rmsR"]], writes=[scrb(fc)])
                p.op("v", lambda h, f=fc, xa=xcur_ap: h.scalar_tensor_tensor(
                    out=xa[:, f * 256 : (f + 1) * 256], in0=scr(f),
                    scalar=gac.ap()[:, f : f + 1],
                    in1=xa[:, f * 256 : (f + 1) * 256],
                    op0=ALU.mult, op1=ALU.add),
                    reads=[scrb(fc), bufs["modpostB"], xcurb], writes=[xcurb])
            p.op("g", lambda h, t=t0, xa=xcur_ap: h.dma_start(
                out=x2d.rearrange("(c p) x -> p c x", p=128)[:, 0:8, t : t + BLK],
                in_=xa.rearrange("p (c w) -> p c w", c=NCH)[:, 0:8, :]),
                reads=[xcurb], writes=[bufs[f"x2_{b_}"]])
            p.op("g", lambda h, t=t0, xa=xcur_ap: h.dma_start(
                out=x2d.rearrange("(c p) x -> p c x", p=128)[:, 8:16, t : t + BLK],
                in_=xa.rearrange("p (c w) -> p c w", c=NCH)[:, 8:16, :]),
                reads=[xcurb], writes=[bufs[f"x2_{b_}"]])

        p.barrier()
        if stages in ("all", "attn", "fold"):
            for uc in range(HCH):
                foldA_iter(uc)
                emit_split_slab(w1, uc, w1_s)
            for vg in range(64):
                foldB_iter(vg)
                emit_split_slab(w3, vg, w3_s)
            bufs["w1S"].ready = [("gs0", p.cnt["gs0"]), ("gs1", p.cnt["gs1"])]
            bufs["w3S"].ready = [("gs0", p.cnt["gs0"]), ("gs1", p.cnt["gs1"])]
            bufs["w2fS"].ready = [("gs2", p.cnt["gs2"]), ("gs3", p.cnt["gs3"])]

        p.barrier()

        if stages == "all":
            load_x(0, 0, x2d, x2buf=bufs["x2_0"])
            emit_ln_stats(xb.ap(), bufs["xb0"], par7=True)
        hm_next = None
        for st in (range(NBLK) if stages == "all" else range(0)):
            t0 = st * BLK
            par = st % 2
            xcur = xb if par == 0 else xbB
            xcur_ap = xcur.ap()
            xcurb = bufs[f"xb{par}"]
            hmarks = hm_next if st > 0 else emit_ln_apply(
                xcur_ap, xcurb, cbpc, 48, mbufs=[bufs["modpostB"], bufs["modctB"]])

            for half in range(2):
                for oc in range(32):
                    for wsel in range(2):
                        srcs = w1_s if wsel == 0 else w3_s
                        wsb = bufs["w1S" if wsel == 0 else "w3S"]
                        slot = (oc * 2 + wsel) % 4
                        wv, wb, lds = load_oc(slot, srcs, half * 32 + oc, wsb)
                        pk = pb[wsel * 2 + (oc % 2)]
                        pkb = bufs[f"pb{wsel * 2 + (oc % 2)}"]
                        if half == 0 and oc == 0 and wsel == 0:
                            mm_group_tracked(pk, pkb, wv, wb, lds, hmarks)
                        else:
                            p.mm_group(fns3_optA(pk, wv), reads=[wb, bufs["hs"]], writes=[pkb], extra=lds)
                        if wsel == 0:
                            p.op("s", lambda h, o=oc, pk_=pk: h.activation(
                                out=A_t[:, o * 256 : (o + 1) * 256], in_=pk_.ap()[:, 0:256],
                                func=AF.Silu, scale=IWS), reads=[pkb], writes=[bufs["A"]])
                        else:
                            p.op("v", lambda h, o=oc, pk_=pk: h.tensor_copy(
                                out=B_t[:, o * 256 : (o + 1) * 256], in_=pk_.ap()[:, 0:256]),
                                reads=[pkb], writes=[bufs["Bf"]])
                    p.op("v", lambda h, o=oc: h.tensor_mul(
                        out=A_t[:, o * 256 : (o + 1) * 256],
                        in0=A_t[:, o * 256 : (o + 1) * 256],
                        in1=B_t[:, o * 256 : (o + 1) * 256]),
                        reads=[bufs["A"], bufs["Bf"]], writes=[bufs["A"]])

                if half == 1 and st + 1 < NBLK:
                    load_x((st + 1) % 2, st + 1, x2d, x2buf=bufs[f"x2_{st + 1}"])
                    nxt = xb if (st + 1) % 2 == 0 else xbB
                    emit_ln_stats(nxt.ap(), bufs[f"xb{(st + 1) % 2}"], par7=True)
                    hm_next = emit_ln_apply(nxt.ap(), bufs[f"xb{(st + 1) % 2}"], cbpc, 48,
                                            mbufs=[bufs["modpostB"], bufs["modctB"]])

                aspg = []
                for g4 in range(8):
                    s4 = slice(g4 * 1024, (g4 + 1) * 1024)
                    p.op("s", lambda h, s=s4: h.copy(out=Ah_t[:, s], in_=A_t[:, s]),
                         reads=[bufs["A"]], writes=[bufs["Bf"]])
                    m_ = p.op("v", lambda h, s=s4: h.tensor_sub(out=Al_t[:, s], in0=A_t[:, s],
                                                                in1=Ah_t[:, s]),
                              reads=[bufs["A"], bufs["Bf"]], writes=[bufs["Asp"], bufs["Bf"]])
                    aspg.append(m_)

                for wave in range(2):
                    wavebanks = [bufs[f"pb{i}"] for i in range(8)]
                    p.deps("t", reads=[bufs["w2fS"]], writes=wavebanks)
                    for uc in range(32):
                        slot = uc % 4
                        off = (slot % 2) * 4096
                        shb = wsl16[slot // 2].ap()[:, off : off + 2048]
                        shbb = bufs[SLOTB[slot]]
                        r0 = half * 4096 + uc * 128
                        ldm2 = p.op("q", lambda h, d=shb, r=r0, w=wave: h.dma_start(
                            out=d.rearrange("p (pl c) -> p pl c", pl=2),
                            in_=w2f_s.rearrange("r (pl c) -> r pl c", pl=2)[r : r + 128, :, w * 1024 : w * 1024 + 1024]),
                            reads=[bufs["w2fS"]], writes=[shbb], sem=f"qw{slot}")
                        p._wait("t", ldm2)
                        p._wait("t", aspg[uc // 4])
                        us_ = slice(uc * 256, (uc + 1) * 256)
                        for fcl in range(8):
                            w0 = fcl * 128
                            p.raw("t", lambda h, d=shb, w=w0, s=us_, f=fcl, u=uc: h.matmul(
                                pb[f].ap()[:, 0:256], d[:, w : w + 128], Ah_t[:, s],
                                start=(u == 0), stop=False))
                            p.raw("t", lambda h, d=shb, w=w0, s=us_, f=fcl: h.matmul(
                                pb[f].ap()[:, 0:256], d[:, w : w + 128], Al_t[:, s],
                                start=False, stop=False))
                            fn = (lambda h, d=shb, w=w0, s=us_, f=fcl, u=uc: h.matmul(
                                pb[f].ap()[:, 0:256], d[:, 1024 + w : 1024 + w + 128], Ah_t[:, s],
                                start=False, stop=(u == 31)))
                            if fcl == 7:
                                m = p.inc("t", fn)
                                shbb.mark_user(m)
                                if uc == 31:
                                    lastmark = m
                            else:
                                p.raw("t", fn)
                        if uc == 31:
                            for i in range(8):
                                bufs[f"pb{i}"].mark_ready(lastmark)
                    bufs["Asp"].mark_user(lastmark)
                    for fcl in range(8):
                        fc = wave * 8 + fcl
                        if half == 0:
                            p.op("v", lambda h, f=fc, fl=fcl: h.tensor_copy(
                                out=ya_t[:, f * 256 : (f + 1) * 256], in_=pb[fl].ap()[:, 0:256]),
                                reads=[bufs[f"pb{fcl}"]], writes=[bufs["ya"]])
                        else:
                            p.op("v", lambda h, f=fc, fl=fcl: h.tensor_add(
                                out=ya_t[:, f * 256 : (f + 1) * 256],
                                in0=ya_t[:, f * 256 : (f + 1) * 256], in1=pb[fl].ap()[:, 0:256]),
                                reads=[bufs[f"pb{fcl}"], bufs["ya"]], writes=[bufs["ya"]])

            for fc in range(NCH):
                p.op("v", lambda h, f=fc, xa=xcur_ap: h.scalar_tensor_tensor(
                    out=ya_t[:, f * 256 : (f + 1) * 256], in0=ya_t[:, f * 256 : (f + 1) * 256],
                    scalar=gbc.ap()[:, f : f + 1], in1=xa[:, f * 256 : (f + 1) * 256],
                    op0=ALU.mult, op1=ALU.add),
                    reads=[bufs["ya"], bufs["modpostB"], xcurb], writes=[bufs["ya"]])
            p.op("g", lambda h, t=t0: h.dma_start(
                out=yT.rearrange("(c p) x -> p c x", p=128)[:, 0:8, t : t + BLK],
                in_=ya_t.rearrange("p (c w) -> p c w", c=NCH)[:, 0:8, :]),
                reads=[bufs["ya"]], writes=[])
            p.op("g", lambda h, t=t0: h.dma_start(
                out=yT.rearrange("(c p) x -> p c x", p=128)[:, 8:16, t : t + BLK],
                in_=ya_t.rearrange("p (c w) -> p c w", c=NCH)[:, 8:16, :]),
                reads=[bufs["ya"]], writes=[])


        if stages != "all":
            p.op("g", lambda h: h.dma_start(out=yT[0:128, 0:96], in_=modct.ap()[:]),
                 reads=[bufs["modct"], bufs["modpost"], bufs["modctB"], bufs["modpostB"]], writes=[])
        gfin = p.cnt["g"]
        p.ops["g"].append(("w", "g", gfin))
        p.ops["g"].append(("w", "gs0", p.cnt["gs0"]))
        p.ops["g"].append(("w", "gs1", p.cnt["gs1"]))
        p.ops["g"].append(("w", "gs2", p.cnt["gs2"]))
        p.ops["g"].append(("w", "gs3", p.cnt["gs3"]))

        def run(e, h):
            for item in p.ops[e]:
                if item[0] == "w":
                    h.wait_ge(sems[item[1]], item[2])
                else:
                    inst = item[1](h)
                    if item[2] is not None:
                        inst.then_inc(sems[item[2]], 16 if item[2] in DMA_SEMS else 1)

        @block.tensor
        def _(tensor):
            run("t", tensor)

        @block.vector
        def _(vector):
            run("v", vector)

        @block.scalar
        def _(scalar):
            run("s", scalar)

        @block.sync
        def _(sync):
            run("q", sync)

        @block.gpsimd
        def _(gpsimd):
            run("g", gpsimd)

    return nc


def _hadamard(n):
    h = np.array([[1.0]], dtype=np.float64)
    while h.shape[0] < n:
        h = np.block([[h, h], [h, -h]])
    return h


def make_inputs(x, slope, w_qkv, w_gate, w_out, norm_w, W1, W2, W3, c, wmod, T):
    f32 = np.float32
    x = np.asarray(x, f32)
    nb = x.shape[0]
    wqkvT = np.ascontiguousarray(np.asarray(w_qkv, f32).T)
    o = np.arange(3 * D)
    h_, r, j = o // 384, (o % 384) // 128, o % 128
    new = r * D + h_ * 128 + j
    src = np.empty(3 * D, np.int64)
    src[new] = o
    wq = np.ascontiguousarray(wqkvT[:, src])
    wg = np.ascontiguousarray(np.asarray(w_gate, f32).T)
    wo = np.ascontiguousarray(np.asarray(w_out, f32).T)
    w1 = np.ascontiguousarray(np.asarray(W1, f32).T)
    w3 = np.ascontiguousarray(np.asarray(W3, f32).T)
    w2t = np.ascontiguousarray(np.asarray(W2, f32).T)
    wmod = np.ascontiguousarray(np.asarray(wmod, f32))
    slope = np.asarray(slope, f32).reshape(NH)
    scols = np.ascontiguousarray(np.broadcast_to(slope[None, :], (128, NH)))
    normwc = np.ascontiguousarray(np.asarray(norm_w, f32).reshape(NCH, 128).T)
    ident = np.eye(128, dtype=f32)
    ramp = np.ascontiguousarray(np.broadcast_to(np.arange(1, BLK + 1, dtype=f32)[None, :], (128, BLK)))
    mm, nn = np.meshgrid(np.arange(BLK, dtype=f32), np.arange(128, dtype=f32))
    idx0 = np.where(mm >= nn, mm - nn, 1e9).astype(f32)
    idx1 = np.where(mm >= nn + 128, mm - nn - 128, 1e9).astype(f32)
    kcol = np.ascontiguousarray(np.stack([255.0 - np.arange(128, dtype=f32),
                                          127.0 - np.arange(128, dtype=f32)], axis=1))
    h128 = _hadamard(128).astype(f32)
    h64 = _hadamard(64) / np.sqrt(8192.0)
    h64d = np.zeros((128, 128), np.float64)
    h64d[0:64, 0:64] = h64
    h64d[64:128, 64:128] = h64
    h64d = h64d.astype(f32)
    shared = dict(wq=wq, wg=wg, wo=wo, w1=w1, w3=w3, w2t=w2t, wmod=wmod, scols=scols,
                  normw=normwc, identc=ident, rampc=ramp, idx0c=idx0, idx1c=idx1,
                  kcolc=kcol, h128c=h128, h64dc=h64d)
    c = np.asarray(c, f32)
    in_maps = []
    for b_ in range(nb):
        m = dict(shared)
        m["xT"] = np.ascontiguousarray(x[b_, :T, :].T)
        m["cT"] = np.ascontiguousarray(c[b_].reshape(NCH, 128).T)
        in_maps.append(m)
    return in_maps


def kernel(L0_attn_hidden_states, L0_attn_slope_rate, L0_attn_w_qkv, L0_attn_w_gate,
           L0_attn_w_out, L0_attn_norm_w, L0_ffn_W1, L0_ffn_W2, L0_ffn_W3, L0_c, L0_wmod):
    T = 2048
    in_maps = make_inputs(L0_attn_hidden_states, L0_attn_slope_rate, L0_attn_w_qkv,
                          L0_attn_w_gate, L0_attn_w_out, L0_attn_norm_w,
                          L0_ffn_W1, L0_ffn_W2, L0_ffn_W3, L0_c, L0_wmod, T)
    nc = build(T)
    res = run_bass_kernel_spmd(nc, in_maps, core_ids=list(range(8)))
    out = np.stack([np.ascontiguousarray(r["yT"].T) for r in res.results], axis=0)
    return out.astype(np.float32)
